# version 01
# baseline (speedup 1.0000x reference)
import numpy as np
import ml_dtypes
from contextlib import ExitStack

import concourse.bass as bass
import concourse.mybir as mybir
from concourse.bass_utils import run_bass_kernel_spmd

F32 = mybir.dt.float32
BF16 = mybir.dt.bfloat16
F16 = mybir.dt.float16
AF = mybir.ActivationFunctionType
ALU = mybir.AluOpType
AX = mybir.AxisListType

B, S, H = 4, 4096, 1024
NH, HD = 16, 64
FF = 4096
LR = 64
T = 2048
NCK = 8
NFT = FF // 128
TBW = 512
NTB = T // TBW
NPT = 8
NST = 8
NS = 1024
SLO = 512
MAGIC = 12582912.0
EPSQ = 1e-8
EPSLN = 1e-5
NEGBIG = -60.0

_DEBUG = False
_FFN2TBS = NTB
_STOP = None

_H4 = np.array([[1, 1, 1, 1], [1, -1, 1, -1], [1, 1, -1, -1], [1, -1, -1, 1]],
               dtype=np.float64)


def _blk16():
    m = np.zeros((128, 128))
    for pt in range(8):
        m[0:64, pt * 16 + 2 * pt] = 1.0
        m[64:128, pt * 16 + 2 * pt + 1] = 1.0
    return m


def _host_consts(z):
    tok0 = 2048 * z
    tg = np.arange(tok0, tok0 + T, dtype=np.float64)
    inv_freq = 10000.0 ** (-(2.0 * np.arange(32, dtype=np.float64) / HD))
    emb = tg[:, None] * inv_freq[None, :]
    emb = np.concatenate([emb, emb], axis=1)
    cos, sin = np.cos(emb), np.sin(emb)
    jj = np.arange(HD)
    phi = _H4[(tg.astype(np.int64) % 4)][:, (jj % 4)]
    sgn = np.where(jj < 32, -1.0, 1.0)[None, :]

    def fm(x):
        a = np.ascontiguousarray(x.T)
        return np.concatenate([a, a], axis=0)

    bf = ml_dtypes.bfloat16
    ds = np.arange(128)[:, None]
    dt = np.arange(512)[None, :]
    real_past = (ds < (dt // 4)).astype(np.float64)
    real_eq = (ds == (dt // 4)).astype(np.float64)
    noop_past = np.ones((128, 512))
    noop_eq = np.zeros((128, 512))
    if z == 0:
        mA_past, mA_eq, mB_past, mB_eq = real_past, real_eq, noop_past, noop_eq
        slotmask = np.concatenate([np.zeros(SLO), np.full(NS - SLO, NEGBIG)])
    else:
        mA_past, mA_eq, mB_past, mB_eq = noop_past, noop_eq, real_past, real_eq
        slotmask = np.zeros(NS)
    selpair = np.zeros((2, 128))
    selpair[0, :64] = 1.0
    selpair[1, 64:] = 1.0
    return {
        "qC1": fm(cos * phi / 8.0).astype(bf),
        "qC2": fm(sin * sgn * phi / 8.0).astype(bf),
        "kC1": fm(cos * phi / 2.0).astype(bf),
        "kC2": fm(sin * sgn * phi / 2.0).astype(bf),
        "vphi": fm(phi / 2.0).astype(bf),
        "phi2": np.ascontiguousarray(fm(phi * 2.0)[:64]).astype(bf),
        "mA_past": mA_past.astype(bf), "mA_eq": mA_eq.astype(bf),
        "mB_past": mB_past.astype(bf), "mB_eq": mB_eq.astype(bf),
        "slotmask": slotmask.reshape(1, NS).astype(np.float32),
        "identb": np.eye(128, dtype=bf),
        "identf": np.eye(128, dtype=np.float32),
        "blk2": np.stack([np.repeat([1.0, 0.0], 64), np.repeat([0.0, 1.0], 64)],
                         axis=1).astype(bf),
        "blk16": _blk16().astype(bf),
        "selp": np.ascontiguousarray(selpair.astype(np.float32)),
        "onesf": np.ones((128, 128), dtype=np.float32),
        "midb": np.full((128, 1), (-100.0 if z == 0 else 0.0), dtype=np.float32),
        "oneh": np.ascontiguousarray(np.repeat(np.eye(16), 128, axis=1)).astype(bf),
        "onescol": np.ones((128, 1), dtype=bf),
    }


class _NullBlk:
    def __getattr__(self, name):
        def deco(f):
            return None
        return deco


class _Sem:
    def __init__(self, ctx, nc, name):
        self.h = ctx.enter_context(nc.semaphore(name))
        self.n = 0

    def inc(self, inst, k=1):
        inst.then_inc(self.h, k)
        self.n += k
        return inst


def build_nc():
    nc = bass.Bass("TRN2", target_bir_lowering=False, detect_race_conditions=False)
    ctx = ExitStack()

    def din(name, shape, dt=F32):
        return nc.dram_tensor(name, shape, dt, kind="ExternalInput").ap()

    xT = din("xT", [H, T])
    WQ, WK, WV, WO = (din(n, [H, H]) for n in ("Wq", "Wk", "Wv", "Wo"))
    AQ, AV_, AO = (din(n, [H, LR]) for n in ("Aq", "Av", "Ao"))
    BQ, BV_, BO = (din(n, [LR, H]) for n in ("Bq", "Bv", "Bo"))
    w1 = din("w1", [FF, H])
    w2 = din("w2", [H, FF])
    b1l = din("b1l", [128, NFT])
    b2l = din("b2l", [128, NCK])
    CN = {}
    for n, sh, dt in (("qC1", [128, T], BF16), ("qC2", [128, T], BF16),
                      ("kC1", [128, T], BF16), ("kC2", [128, T], BF16),
                      ("vphi", [128, T], BF16), ("phi2", [64, T], BF16),
                      ("mA_past", [128, 512], BF16), ("mA_eq", [128, 512], BF16),
                      ("mB_past", [128, 512], BF16), ("mB_eq", [128, 512], BF16),
                      ("slotmask", [1, NS], F32), ("identb", [128, 128], BF16),
                      ("identf", [128, 128], F32), ("blk2", [128, 2], BF16),
                      ("blk16", [128, 128], BF16),
                      ("selp", [2, 128], F32), ("oneh", [16, NH * 128], BF16),
                      ("midb", [128, 1], F32),
                      ("onesf", [128, 128], F32), ("onescol", [128, 1], BF16)):
        CN[n] = din(n, sh, dt)

    yT = nc.dram_tensor("yT", [H, T], F32, kind="ExternalOutput").ap()

    okind = {}
    w1c_d = nc.dram_tensor("w1c_d", [FF, H], BF16, **okind).ap()
    w2c_d = nc.dram_tensor("w2c_d", [H, FF], BF16, **okind).ap()
    h_d = nc.dram_tensor("h_d", [NTB, FF, TBW], F16, **okind).ap()
    mx_in = nc.dram_tensor("mx_in", [1, 16], F32).ap()
    mx_out = nc.dram_tensor("mx_out", [1, 16], F32).ap()
    kv_in = nc.dram_tensor("kv_in", [2 * NPT * 128, SLO], BF16).ap()
    kv_out = nc.dram_tensor("kv_out", [2 * 2 * NPT * 128, SLO], BF16, **okind).ap()
    dbgt = {}
    if _DEBUG:
        for n, sh, dt in (("d_a", [128, NCK * T], F32), ("d_b", [128, NCK * T], F32),
                          ("d_c", [128, NCK * T], F32), ("d_d", [128, NCK * T], F32)):
            dbgt[n] = nc.dram_tensor(n, sh, dt, kind="ExternalOutput").ap()

    sbp = lambda name, sh, dt: ctx.enter_context(nc.sbuf_tensor(name, sh, dt))

    Rsb = sbp("Rsb", [128, NCK * T], F32)
    pool2 = sbp("pool2", [128, T], F32)
    csp = sbp("csp", [16, T], F32)
    colsc = sbp("colsc", [128, 16], F32)
    b1sb = sbp("b1sb", [128, NFT], F32)
    b2sb = sbp("b2sb", [128, NCK], F32)
    am1 = sbp("am1", [128, NFT], F32)
    am2 = sbp("am2", [128, NCK], F32)
    sc1 = sbp("sc1", [128, NFT], F32)
    bi1 = sbp("bi1", [128, NFT], F32)
    iv1 = sbp("iv1", [128, NFT], F32)
    sc2 = sbp("sc2", [128, NCK], F32)
    bi2 = sbp("bi2", [128, NCK], F32)
    iv2 = sbp("iv2", [128, NCK], F32)
    qscr = sbp("qscr", [128, NFT], F32)
    csb = {}
    for n in ("identb", "identf", "blk2", "blk16", "selp", "onesf", "onescol",
              "midb"):
        csb[n] = sbp("c_" + n, list(CN[n].shape), CN[n].dtype)
    chead = sbp("chead", [16, 4], F32)
    nc16 = sbp("nc16", [16, 1], BF16)
    ncf = sbp("ncf", [16, 1], F32)
    ch8 = sbp("ch8", [16, 8], F32)
    vnop = sbp("vnop", [128, 2], F32)

    def vsp(vector, n=2):
        for _ in range(n):
            vector.tensor_copy(out=vnop[:, 0:1], in_=vnop[:, 1:2])



    pb = ctx.enter_context(nc.psum_tensor("pb", [128, 4096], F32))

    def bank(i, w=TBW):
        return pb[:, i * TBW:i * TBW + w]

    sL = _Sem(ctx, nc, "sL")
    sG = _Sem(ctx, nc, "sG")
    sS = _Sem(ctx, nc, "sS")
    sV = _Sem(ctx, nc, "sV")
    sT = _Sem(ctx, nc, "sT")
    sC = _Sem(ctx, nc, "sC")
    sM = _Sem(ctx, nc, "sM")
    sE = _Sem(ctx, nc, "sE")
    sX = _Sem(ctx, nc, "sX")
    sR = _Sem(ctx, nc, "sR")
    sK = _Sem(ctx, nc, "sK")

    blk = ctx.enter_context(nc.Block())

    @blk.sync
    def _(sync):
        for c in range(NCK):
            sL.inc(sync.dma_start(out=Rsb[:, c * T:(c + 1) * T],
                                  in_=xT[c * 128:(c + 1) * 128, :]), 16)
        for n in csb:
            sL.inc(sync.dma_start(out=csb[n][:, :], in_=CN[n][:, :]), 16)

        sL.inc(sync.dma_start(out=b1sb[:, :], in_=b1l[:, :]), 16)
        sL.inc(sync.dma_start(out=b2sb[:, :], in_=b2l[:, :]), 16)
    p0_loads = sL.n

    @blk.vector
    def _(vector):
        vector.memset(pool2[0:1, :], 0.0)
        vector.memset(pool2[64:65, 0:16], 0.0)
        vector.memset(colsc[:, :], 0.0)
        vector.memset(colsc[:, 8:9], MAGIC)
        sV.inc(vector.memset(colsc[:, 9:10], EPSLN))
    p0_done_v = sV.n

    def emit_wquant(baseG_wcast):
        jobs = []
        for ft in range(NFT):
            jobs.append((w1[ft * 128:(ft + 1) * 128, :], am1[:, ft:ft + 1], True,
                         w1c_d[ft * 128:(ft + 1) * 128, :], iv1[:, ft:ft + 1]))
        for dc in range(NCK):
            for sub in range(4):
                jobs.append((w2[dc * 128:(dc + 1) * 128, sub * 1024:(sub + 1) * 1024],
                             am2[:, dc:dc + 1], sub == 0,
                             w2c_d[dc * 128:(dc + 1) * 128, sub * 1024:(sub + 1) * 1024],
                             iv2[:, dc:dc + 1]))
        nj = len(jobs)
        with (nc.sbuf_tensor("wst", [128, 2 * 1024], F32) as wst,
              nc.sbuf_tensor("cst", [128, 2 * 1024], BF16) as cst,
              nc.sbuf_tensor("ust0", [128, 2 * 1024], F32) as ust0):
            baseL, baseV = sL.n, sV.n
            @blk.sync
            def _(sync):
                sync.wait_ge(sG.h, baseG_wcast)
                for i, (src, amc, first, dst, ivc) in enumerate(jobs):
                    if i >= 2:
                        sync.wait_ge(sV.h, baseV + i - 1)
                    sL.inc(sync.dma_start(out=wst[:, (i % 2) * 1024:(i % 2 + 1) * 1024],
                                          in_=src), 16)

            @blk.vector
            def _(vector):
                for i, (src, amc, first, dst, ivc) in enumerate(jobs):
                    vector.wait_ge(sL.h, baseL + (i + 1) * 16)
                    wsl = wst[:, (i % 2) * 1024:(i % 2 + 1) * 1024]
                    if first:
                        sV.inc(vector.reduce_max(out=amc, in_=wsl, axis=AX.X,
                                                 apply_absolute_value=True))
                    else:
                        vector.reduce_max(out=colsc[:, 0:1], in_=wsl, axis=AX.X,
                                          apply_absolute_value=True)
                        vsp(vector)
                        sV.inc(vector.tensor_tensor(out=amc, in0=amc,
                                                    in1=colsc[:, 0:1], op=ALU.max))
                vector.tensor_single_scalar(out=am1[:, :], in_=am1[:, :],
                                            scalar=EPSQ, op=ALU.max)
                vector.tensor_single_scalar(out=am2[:, :], in_=am2[:, :],
                                            scalar=EPSQ, op=ALU.max)
                vsp(vector)
                vector.reciprocal(out=iv1[:, :], in_=am1[:, :])
                vector.reciprocal(out=iv2[:, :], in_=am2[:, :])
                vsp(vector)
                vector.tensor_single_scalar(out=iv1[:, :], in_=iv1[:, :],
                                            scalar=127.0, op=ALU.mult)
                sV.inc(vector.tensor_single_scalar(out=iv2[:, :], in_=iv2[:, :],
                                                   scalar=127.0, op=ALU.mult))
            ivdone = sV.n

            baseL2, baseS2, baseV2, baseG2 = sL.n, sS.n, sV.n, sG.n

            @blk.sync
            def _(sync):
                sync.wait_ge(sV.h, ivdone)
                for i, (src, amc, first, dst, ivc) in enumerate(jobs):
                    if i >= 2:
                        sync.wait_ge(sV.h, baseV2 + i - 1)
                    sL.inc(sync.dma_start(out=wst[:, (i % 2) * 1024:(i % 2 + 1) * 1024],
                                          in_=src), 16)

            @blk.scalar
            def _(scalar):
                for i, (src, amc, first, dst, ivc) in enumerate(jobs):
                    scalar.wait_ge(sL.h, baseL2 + (i + 1) * 16)
                    if i >= 2:
                        scalar.wait_ge(sV.h, baseV2 + i - 1)
                    sS.inc(scalar.activation(
                        out=ust0[:, (i % 2) * 1024:(i % 2 + 1) * 1024],
                        in_=wst[:, (i % 2) * 1024:(i % 2 + 1) * 1024],
                        func=AF.Identity, scale=ivc, bias=colsc[:, 8:9]))

            @blk.vector
            def _(vector):
                for i, (src, amc, first, dst, ivc) in enumerate(jobs):
                    vector.wait_ge(sS.h, baseS2 + i + 1)
                    if i >= 2:
                        vector.wait_ge(sG.h, baseG2 + 16 * (i - 1))
                    sV.inc(vector.tensor_single_scalar(
                        out=cst[:, (i % 2) * 1024:(i % 2 + 1) * 1024],
                        in_=ust0[:, (i % 2) * 1024:(i % 2 + 1) * 1024],
                        scalar=MAGIC, op=ALU.subtract))

            @blk.gpsimd
            def _(g):
                for i, (src, amc, first, dst, ivc) in enumerate(jobs):
                    g.wait_ge(sV.h, baseV2 + i + 1)
                    sG.inc(g.dma_start(out=dst,
                                       in_=cst[:, (i % 2) * 1024:(i % 2 + 1) * 1024]), 16)
        return sG.n, sV.n

    def emit_allreduce_max(vwait):
        @blk.gpsimd
        def _(g):
            g.wait_ge(sV.h, vwait)
            sG.inc(g.dma_start(out=mx_in[:, :], in_=pool2[64:65, 0:16]), 16)
            g.wait_ge(sG.h, sG.n)
            sC.inc(g.collective_compute(
                "AllReduce", ALU.max, replica_groups=[list(range(8))],
                ins=[mx_in], outs=[mx_out]))
            g.wait_ge(sC.h, sC.n)
            sG.inc(g.dma_start(out=pool2[64:65, 1:2], in_=mx_out[0:1, 0:1]), 16)
        return sG.n

    def emit_ln_stats_affine(pool1, lnb, sqb, lnscr, lnfull, xcode, quant):
        baseS = sS.n
        baseT = sT.n
        gateL, gateG, gateV = sL.n, sG.n, sV.n
        @blk.scalar
        def _(scalar):
            scalar.wait_ge(sL.h, gateL)
            scalar.wait_ge(sG.h, gateG)
            scalar.wait_ge(sV.h, gateV)
            for dc in range(NCK):
                if dc >= 2:
                    scalar.wait_ge(sT.h, baseT + dc - 1)
                scalar.copy(out=lnb[:, (dc % 2) * T:(dc % 2 + 1) * T],
                            in_=Rsb[:, dc * T:(dc + 1) * T])
                sS.inc(scalar.activation(out=sqb[:, (dc % 2) * T:(dc % 2 + 1) * T],
                                         in_=lnb[:, (dc % 2) * T:(dc % 2 + 1) * T],
                                         func=AF.Square))
        @blk.tensor
        def _(tensor):
            for dc in range(NCK):
                tensor.wait_ge(sS.h, baseS + dc + 1)
                for tb in range(NTB):
                    tensor.matmul(pb[0:1, tb * TBW:(tb + 1) * TBW],
                                  csb["onescol"][:, :],
                                  lnb[:, (dc % 2) * T + tb * TBW:(dc % 2) * T + (tb + 1) * TBW],
                                  start=(dc == 0), stop=(dc == NCK - 1))
                for tb in range(NTB):
                    inst = tensor.matmul(pb[0:1, (4 + tb) * TBW:(5 + tb) * TBW],
                                         csb["onescol"][:, :],
                                         sqb[:, (dc % 2) * T + tb * TBW:(dc % 2) * T + (tb + 1) * TBW],
                                         start=(dc == 0), stop=(dc == NCK - 1))
                    if tb == NTB - 1:
                        sT.inc(inst)
        statsT = sT.n

        baseV = sV.n
        @blk.vector
        def _(vector):
            vector.wait_ge(sT.h, statsT)
            for tb in range(NTB):
                vector.tensor_scalar(out=csp[0:1, tb * TBW:(tb + 1) * TBW],
                                     in0=pb[0:1, tb * TBW:(tb + 1) * TBW],
                                     scalar1=1.0 / 1024.0, scalar2=None, op0=ALU.mult)
                vector.tensor_scalar(out=pool1[96:97, tb * TBW:(tb + 1) * TBW],
                                     in0=pb[0:1, (4 + tb) * TBW:(5 + tb) * TBW],
                                     scalar1=1.0 / 1024.0, scalar2=None, op0=ALU.mult)
            vector.tensor_tensor(out=pool2[96:97, :], in0=csp[0:1, :].copy() if False
                                 else pool2[96:97, :], in1=pool2[96:97, :], op=ALU.bypass)                 if False else None
            vector.tensor_copy(out=pool2[96:97, :], in_=csp[0:1, :])
            vector.tensor_tensor(out=pool2[96:97, :], in0=pool2[96:97, :],
                                 in1=pool2[96:97, :], op=ALU.mult)
            sV.inc(vector.tensor_tensor(out=pool1[96:97, :], in0=pool1[96:97, :],
                                        in1=pool2[96:97, :], op=ALU.subtract))
        varV = sV.n

        @blk.scalar
        def _(scalar):
            scalar.wait_ge(sV.h, varV)
            sS.inc(scalar.activation(out=pool2[96:97, :], in_=pool1[96:97, :],
                                     func=AF.Sqrt, bias=colsc[96:97, 9:10]))
        sdS = sS.n

        @blk.vector
        def _(vector):
            vector.wait_ge(sS.h, sdS)
            vector.reciprocal(out=pool1[0:1, :], in_=pool2[96:97, :])
            vector.tensor_tensor(out=pool2[0:1, :], in0=csp[0:1, :],
                                 in1=pool1[0:1, :], op=ALU.mult)
            sV.inc(vector.tensor_scalar(out=pool2[0:1, :],
                                        in0=pool2[0:1, :], scalar1=-1.0,
                                        scalar2=None, op0=ALU.mult))
        abV = sV.n

        @blk.tensor
        def _(tensor):
            tensor.wait_ge(sV.h, abV)
            for tb in range(NTB):
                tensor.matmul(bank(tb), csb["onesf"][0:1, :],
                              pool1[0:1, tb * TBW:(tb + 1) * TBW],
                              start=True, stop=True)
                inst = tensor.matmul(bank(4 + tb), csb["onesf"][0:1, :],
                                     pool2[0:1, tb * TBW:(tb + 1) * TBW],
                                     start=True, stop=True)
                if tb == NTB - 1:
                    sT.inc(inst)
        bcT = sT.n

        @blk.vector
        def _(vector):
            vector.wait_ge(sT.h, bcT)
            for dc in range(NCK):
                for tb in range(NTB):
                    dst = (lnfull[:, dc * T + tb * TBW:dc * T + (tb + 1) * TBW]
                           if quant else lnscr[:, 0:TBW])
                    vector.tensor_tensor(out=dst,
                                         in0=Rsb[:, dc * T + tb * TBW:dc * T + (tb + 1) * TBW],
                                         in1=bank(tb), op=ALU.mult)
                    out2 = (dst if quant else
                            xcode[:, dc * T + tb * TBW:dc * T + (tb + 1) * TBW])
                    vector.tensor_tensor(out=out2, in0=dst, in1=bank(4 + tb),
                                         op=ALU.add)
                if quant:
                    vector.reduce_max(out=colsc[:, 0:1],
                                      in_=lnfull[:, dc * T:(dc + 1) * T], axis=AX.X)
                    vsp(vector)
                    if dc == 0:
                        vector.tensor_copy(out=colsc[:, 2:3], in_=colsc[:, 0:1])
                    else:
                        vector.tensor_tensor(out=colsc[:, 2:3], in0=colsc[:, 2:3],
                                             in1=colsc[:, 0:1], op=ALU.max)
            if quant:
                vsp(vector)
            sV.inc(vector.engine_nop() if False else
                   vector.tensor_copy(out=colsc[:, 1:2], in_=colsc[:, 2:3]))
        affV = sV.n

        if not quant:
            return affV

        @blk.tensor
        def _(tensor):
            tensor.wait_ge(sV.h, affV)
            sT.inc(tensor.transpose(pb[0:1, 0:128], colsc[:, 1:2], csb["identf"][:, :]))
        trT = sT.n

        @blk.vector
        def _(vector):
            vector.wait_ge(sT.h, trT)
            sV.inc(vector.reduce_max(out=pool2[64:65, 0:1], in_=pb[0:1, 0:128], axis=AX.X))
        return sV.n

    def emit_ln_pass2(lnfull, xcode, gwait):
        @blk.vector
        def _(vector):
            vector.wait_ge(sG.h, gwait)
            vector.tensor_scalar(out=pool2[64:65, 2:3], in0=pool2[64:65, 1:2], scalar1=EPSQ,
                                 scalar2=1.0 / 255.0, op0=ALU.max, op1=ALU.mult)
            vsp(vector)
            sV.inc(vector.reciprocal(out=pool2[64:65, 3:4], in_=pool2[64:65, 2:3]))
        invV = sV.n

        @blk.tensor
        def _(tensor):
            tensor.wait_ge(sV.h, invV)
            sT.inc(tensor.matmul(pb[:, 0:1], csb["onesf"][64:65, :], pool2[64:65, 3:4],
                                 start=True, stop=True))
        bcT = sT.n

        @blk.vector
        def _(vector):
            vector.wait_ge(sT.h, bcT)
            vector.tensor_copy(out=colsc[:, 3:4], in_=pb[:, 0:1])
            vsp(vector)
            for dc in range(NCK):
                vector.tensor_scalar(out=lnfull[:, dc * T:(dc + 1) * T],
                                     in0=lnfull[:, dc * T:(dc + 1) * T],
                                     scalar1=colsc[:, 3:4], scalar2=MAGIC,
                                     op0=ALU.mult, op1=ALU.add)
                sV.inc(vector.tensor_scalar(out=xcode[:, dc * T:(dc + 1) * T],
                                            in0=lnfull[:, dc * T:(dc + 1) * T],
                                            scalar1=MAGIC, scalar2=0.0,
                                            op0=ALU.subtract, op1=ALU.max))
        return sV.n

    def emit_ffn1(w1sb, hst, xcode, gwait_codes, gwait_scales):
        baseVp = sV.n
        @blk.vector
        def _(vector):
            vector.wait_ge(sV.h, gwait_scales)
            sV.inc(vector.tensor_scalar(out=pool2[64:65, 5:6], in0=pool2[64:65, 2:3],
                                        scalar1=1.0 / 127.0, scalar2=None, op0=ALU.mult))
        s1V = sV.n

        @blk.tensor
        def _(tensor):
            tensor.wait_ge(sV.h, s1V)
            sT.inc(tensor.matmul(pb[:, 0:1], csb["onesf"][64:65, :], pool2[64:65, 5:6],
                                 start=True, stop=True))
        bcT = sT.n

        @blk.vector
        def _(vector):
            vector.wait_ge(sT.h, bcT)
            vector.tensor_copy(out=colsc[:, 5:6], in_=pb[:, 0:1])
            vsp(vector)
            vector.tensor_scalar(out=sc1[:, :], in0=am1[:, :], scalar1=colsc[:, 5:6],
                                 scalar2=None, op0=ALU.mult)
            vsp(vector)
            vector.reciprocal(out=qscr[:, :], in_=sc1[:, :])
            vsp(vector)
            vector.tensor_tensor(out=qscr[:, :], in0=b1sb[:, :], in1=qscr[:, :],
                                 op=ALU.mult)
            vsp(vector)
            vector.tensor_scalar(out=qscr[:, :], in0=qscr[:, :], scalar1=MAGIC,
                                 scalar2=MAGIC, op0=ALU.add, op1=ALU.subtract)
            vsp(vector)
            sV.inc(vector.tensor_tensor(out=bi1[:, :], in0=qscr[:, :], in1=sc1[:, :],
                                        op=ALU.mult))
        prepV = sV.n

        baseLw = sL.n
        @blk.sync
        def _(sync):
            sync.wait_ge(sG.h, gwait_codes)
            for dc in range(NCK):
                sL.inc(sync.dma_start_transpose(w1sb[:, dc * FF:(dc + 1) * FF],
                                                w1c_d[:, dc * 128:(dc + 1) * 128]), 16)
        wldL = sL.n

        baseM, baseE, baseG, baseV2 = sM.n, sE.n, sG.n, sV.n
        @blk.tensor
        def _(tensor):
            tensor.wait_ge(sL.h, wldL)
            tensor.wait_ge(sV.h, prepV)
            for ft in range(NFT):
                if ft >= 2:
                    tensor.wait_ge(sE.h, baseE + 4 * (ft - 1))
                    tensor.wait_ge(sV.h, baseV2 + 4 * (ft - 1))
                st = (ft % 2) * 4
                for dc in range(NCK):
                    lhsT = w1sb[:, dc * FF + ft * 128:dc * FF + (ft + 1) * 128]
                    for tb in range(NTB):
                        inst = tensor.matmul(
                            bank(st + tb), lhsT,
                            xcode[:, dc * T + tb * TBW:dc * T + (tb + 1) * TBW],
                            start=(dc == 0), stop=(dc == NCK - 1))
                        if dc == NCK - 1 and tb == NTB - 1:
                            sM.inc(inst)

        @blk.scalar
        def _(scalar):
            for ft in range(NFT):
                scalar.wait_ge(sM.h, baseM + ft + 1)
                if ft >= 2:
                    scalar.wait_ge(sG.h, baseG + 16 * (ft - 1))
                    scalar.wait_ge(sV.h, baseV2 + 4 * (ft - 1))
                st = (ft % 2) * 4
                for tb in range(NTB):
                    sE.inc(scalar.activation(
                        out=hst[:, (ft % 2) * T + tb * TBW:(ft % 2) * T + (tb + 1) * TBW],
                        in_=bank(st + tb), func=AF.Relu,
                        scale=sc1[:, ft:ft + 1], bias=bi1[:, ft:ft + 1]))

        @blk.vector
        def _(vector):
            for ft in range(NFT):
                for tb in range(NTB):
                    vector.wait_ge(sE.h, baseE + 4 * ft + tb + 1)
                    vector.reduce_max(
                        out=colsc[:, 0:1],
                        in_=hst[:, (ft % 2) * T + tb * TBW:(ft % 2) * T + (tb + 1) * TBW],
                        axis=AX.X)
                    vsp(vector)
                    if ft == 0 and tb == 0:
                        sV.inc(vector.tensor_copy(out=colsc[:, 6:7], in_=colsc[:, 0:1]))
                    else:
                        sV.inc(vector.tensor_tensor(out=colsc[:, 6:7], in0=colsc[:, 6:7],
                                                    in1=colsc[:, 0:1], op=ALU.max))
        hmaxV = sV.n

        @blk.gpsimd
        def _(g):
            for ft in range(NFT):
                g.wait_ge(sE.h, baseE + 4 * (ft + 1))
                sG.inc(g.dma_start(
                    out=h_d[:, ft * 128:(ft + 1) * 128, :].rearrange("tb p t -> p tb t"),
                    in_=hst[:, (ft % 2) * T:(ft % 2 + 1) * T].rearrange(
                        "p (tb t) -> p tb t", tb=NTB)), 16)
        hstG = sG.n

        @blk.tensor
        def _(tensor):
            tensor.wait_ge(sV.h, hmaxV)
            sT.inc(tensor.transpose(pb[0:1, 0:128], colsc[:, 6:7], csb["identf"][:, :]))
        trT = sT.n

        @blk.vector
        def _(vector):
            vector.wait_ge(sT.h, trT)
            sV.inc(vector.reduce_max(out=pool2[64:65, 0:1], in_=pb[0:1, 0:128], axis=AX.X))
        return sV.n, hstG

    def emit_ffn2(w2sb, hbuf, ust, c2st, fst, gwait_codes, gwait_m2):
        @blk.vector
        def _(vector):
            vector.wait_ge(sG.h, gwait_m2)
            vector.tensor_scalar(out=pool2[64:65, 6:7], in0=pool2[64:65, 1:2], scalar1=EPSQ,
                                 scalar2=1.0 / 255.0, op0=ALU.max, op1=ALU.mult)
            vsp(vector)
            vector.reciprocal(out=pool2[64:65, 7:8], in_=pool2[64:65, 6:7])
            vector.tensor_scalar(out=pool2[64:65, 8:9], in0=pool2[64:65, 6:7],
                                 scalar1=0.5 / 127.0, scalar2=None, op0=ALU.mult)
            sV.inc(vector.tensor_scalar(out=pool2[64:65, 9:10], in0=pool2[64:65, 6:7],
                                        scalar1=1.0 / 127.0, scalar2=None, op0=ALU.mult))
        sclV = sV.n

        @blk.tensor
        def _(tensor):
            tensor.wait_ge(sV.h, sclV)
            tensor.matmul(pb[:, 0:1], csb["onesf"][64:65, :], pool2[64:65, 7:8],
                          start=True, stop=True)
            tensor.matmul(pb[:, 1:2], csb["onesf"][64:65, :], pool2[64:65, 8:9],
                          start=True, stop=True)
            sT.inc(tensor.matmul(pb[:, 2:3], csb["onesf"][64:65, :], pool2[64:65, 9:10],
                                 start=True, stop=True))
        bc2T = sT.n

        @blk.vector
        def _(vector):
            vector.wait_ge(sT.h, bc2T)
            vector.tensor_copy(out=colsc[:, 10:11], in_=pb[:, 1:2])
            vector.tensor_copy(out=colsc[:, 11:12], in_=pb[:, 2:3])
            vector.tensor_copy(out=colsc[:, 7:8], in_=pb[:, 0:1])
            vector.tensor_scalar(out=sc2[:, :], in0=am2[:, :], scalar1=colsc[:, 10:11],
                                 scalar2=None, op0=ALU.mult)
            vector.tensor_scalar(out=iv2[:, 0:NCK], in0=am2[:, :], scalar1=colsc[:, 11:12],
                                 scalar2=None, op0=ALU.mult)
            vsp(vector)
            vector.reciprocal(out=iv2[:, 0:NCK], in_=iv2[:, 0:NCK])
            vsp(vector)
            vector.tensor_tensor(out=bi2[:, :], in0=b2sb[:, :], in1=iv2[:, 0:NCK],
                                 op=ALU.mult)
            vsp(vector)
            vector.tensor_scalar(out=bi2[:, :], in0=bi2[:, :], scalar1=MAGIC,
                                 scalar2=MAGIC, op0=ALU.add, op1=ALU.subtract)
            vsp(vector)
            vector.tensor_tensor(out=bi2[:, :], in0=bi2[:, :], in1=sc2[:, :],
                                 op=ALU.mult)
            sV.inc(vector.tensor_copy(out=colsc[:, 12:13], in_=colsc[:, 7:8]))
        prep2V = sV.n

        @blk.sync
        def _(sync):
            sync.wait_ge(sG.h, gwait_codes)
            for fc in range(NFT):
                sL.inc(sync.dma_start_transpose(w2sb[:, fc * 1024:(fc + 1) * 1024],
                                                w2c_d[:, fc * 128:(fc + 1) * 128]), 16)
        w2L = sL.n

        baseLh, baseS, baseV2 = sL.n, sS.n, sV.n
        baseM2, baseE2, baseX, baseR = sM.n, sE.n, sX.n, sR.n

        @blk.sync
        def _(sync):
            for tb in range(globals()['_FFN2TBS']):
                for fc in range(NFT):
                    i = tb * NFT + fc
                    if i >= 4:
                        tb2, fc2 = divmod(i - 4, NFT)
                        sync.wait_ge(sS.h, baseS + 32 * tb2 + fc2 + 1)
                    sL.inc(sync.dma_start(
                        out=hbuf[:, (i % 4) * TBW:(i % 4 + 1) * TBW],
                        in_=h_d[tb, fc * 128:(fc + 1) * 128, :]), 16)

        for tb in range(globals()['_FFN2TBS']):
            @blk.scalar
            def _(scalar, tb=tb):
                if tb == 0:
                    scalar.wait_ge(sV.h, prep2V)
                for fc in range(NFT):
                    i = tb * NFT + fc
                    if i >= 2:
                        scalar.wait_ge(sV.h, baseV2 + i - 1)
                    scalar.wait_ge(sL.h, baseLh + 16 * (i + 1))
                    sS.inc(scalar.activation(
                        out=ust[:, (i % 2) * TBW:(i % 2 + 1) * TBW],
                        in_=hbuf[:, (i % 4) * TBW:(i % 4 + 1) * TBW],
                        func=AF.Identity, scale=colsc[:, 7:8], bias=colsc[:, 8:9]))

            @blk.vector
            def _(vector, tb=tb):
                for fc in range(NFT):
                    i = tb * NFT + fc
                    vector.wait_ge(sS.h, baseS + 32 * tb + fc + 1)
                    if i >= 2:
                        vector.wait_ge(sX.h, baseX + i - 1)
                    sV.inc(vector.tensor_single_scalar(
                        out=c2st[:, (i % 2) * TBW:(i % 2 + 1) * TBW],
                        in_=ust[:, (i % 2) * TBW:(i % 2 + 1) * TBW],
                        scalar=MAGIC, op=ALU.subtract))

            @blk.tensor
            def _(tensor, tb=tb):
                if tb == 0:
                    tensor.wait_ge(sL.h, w2L)
                    tensor.wait_ge(sV.h, prep2V)
                else:
                    tensor.wait_ge(sE.h, baseE2 + 8 * tb)
                for fc in range(NFT):
                    i = tb * NFT + fc
                    tensor.wait_ge(sV.h, baseV2 + 32 * tb + fc + 1)
                    for dt in range(NCK):
                        inst = tensor.matmul(
                            bank(dt),
                            w2sb[:, fc * 1024 + dt * 128:fc * 1024 + (dt + 1) * 128],
                            c2st[:, (i % 2) * TBW:(i % 2 + 1) * TBW],
                            start=(fc == 0), stop=(fc == NFT - 1))
                        if dt == NCK - 1:
                            sX.inc(inst)
                        elif fc == NFT - 1:
                            sM.inc(inst)

            @blk.scalar
            def _(scalar, tb=tb):
                for dt in range(NCK):
                    if dt < NCK - 1:
                        scalar.wait_ge(sM.h, baseM2 + tb * (NCK - 1) + dt + 1)
                    else:
                        scalar.wait_ge(sX.h, baseX + 32 * (tb + 1))
                    j = tb * NCK + dt
                    if j >= 2:
                        scalar.wait_ge(sR.h, baseR + j - 1)
                    sE.inc(scalar.activation(
                        out=fst[:, (dt % 2) * TBW:(dt % 2 + 1) * TBW], in_=bank(dt),
                        func=AF.Identity, scale=sc2[:, dt:dt + 1], bias=bi2[:, dt:dt + 1]))

            @blk.vector
            def _(vector, tb=tb):
                for dt in range(NCK):
                    vector.wait_ge(sE.h, baseE2 + tb * NCK + dt + 1)
                    sR.inc(vector.tensor_tensor(
                        out=Rsb[:, dt * T + tb * TBW:dt * T + (tb + 1) * TBW],
                        in0=Rsb[:, dt * T + tb * TBW:dt * T + (tb + 1) * TBW],
                        in1=fst[:, (dt % 2) * TBW:(dt % 2 + 1) * TBW], op=ALU.add))
        return sR.n

    Wb16 = {n: nc.dram_tensor(n + "b16", [H, H], BF16).ap()
            for n in ("Wq", "Wk", "Wv", "Wo")}
    o2_d = nc.dram_tensor("o2_d", [H, T], BF16).ap()

    def emit_wcast():
        with nc.sbuf_tensor("wtmp", [128, 2 * 1024], BF16) as wtmp:
            srcs = [("Wq", WQ), ("Wk", WK), ("Wv", WV), ("Wo", WO)]
            baseG = sG.n
            @blk.gpsimd
            def _(g):
                i = 0
                for nm, Wap in srcs:
                    for dc in range(NCK):
                        if i >= 2:
                            g.wait_ge(sG.h, baseG + 32 * (i - 1))
                        sG.inc(g.dma_start(out=wtmp[:, (i % 2) * 1024:(i % 2 + 1) * 1024],
                                           in_=Wap[dc * 128:(dc + 1) * 128, :]), 16)
                        g.wait_ge(sG.h, baseG + 32 * i + 16)
                        sG.inc(g.dma_start(out=Wb16[nm][dc * 128:(dc + 1) * 128, :],
                                           in_=wtmp[:, (i % 2) * 1024:(i % 2 + 1) * 1024]), 16)
                        i += 1
        return sG.n

    def emit_proj(pname, ln2V, aq, esA, DcBox, xcode):
        Wb = Wb16[{"q": "Wq", "k": "Wk", "v": "Wv"}[pname]]
        Aap = {"q": AQ, "k": None, "v": AV_}[pname]
        Bap = {"q": BQ, "k": None, "v": BV_}[pname]
        tA = {"q": CN["qC1"], "k": CN["kC1"], "v": CN["vphi"]}[pname]
        tB = {"q": CN["qC2"], "k": CN["kC2"], "v": None}[pname]
        es_p = ExitStack()
        sb3 = lambda name, sh, dt: es_p.enter_context(
            nc.sbuf_tensor(name + pname, sh, dt))
        wring = sb3("wring", [128, 2 * 1024], BF16)
        tbl1 = sb3("tbl1", [128, T], BF16)
        tbl2 = t2s = t3s = absb = bbsb = usb = None
        kshb = PTt = prodb = st4a = st4b = kvt = sqs = None
        if tB is not None:
            tbl2 = sb3("tbl2", [128, T], BF16)
            t2s = sb3("t2s", [128, T], F32)
            t3s = sb3("t3s", [128, T], F32)
        if Aap is not None:
            absb = sb3("absb", [128, NCK * LR], BF16)
            bbsb = sb3("bbsb", [64, 1024], BF16)
            usb = sb3("usb", [64, T], BF16)
        if pname == "q":
            sqs = sb3("sqs", [128, T], BF16)
        if pname in ("k", "v"):
            kshb = sb3("kshb", [128, T], BF16)
            st4a = sb3("st4a", [128, TBW], BF16)
            st4b = sb3("st4b", [128, TBW], BF16)
            kvt = sb3("kvt", [128, TBW], BF16)
        if pname == "k":
            PTt = sb3("PTt", [128, T], BF16)
            prodb = sb3("prodb", [128, T], BF16)
        Dc = DcBox[0]

        @blk.sync
        def _(sync):
            sync.wait_ge(sK.h, sK.n)
            sync.wait_ge(sV.h, ln2V)
            sync.wait_ge(sG.h, sG.n)
            sL.inc(sync.dma_start(out=tbl1[:, :], in_=tA[:, :]), 16)
            if tB is not None:
                sL.inc(sync.dma_start(out=tbl2[:, :], in_=tB[:, :]), 16)
        tblL = sL.n

        if Aap is not None:
            @blk.gpsimd
            def _(g):
                g.wait_ge(sV.h, ln2V)
                for dc in range(NCK):
                    sG.inc(g.dma_start(out=absb[:, dc * LR:(dc + 1) * LR],
                                       in_=Aap[dc * 128:(dc + 1) * 128, :]), 16)
                sG.inc(g.dma_start(out=bbsb[:, :], in_=Bap[:, :]), 16)
            abG = sG.n

            @blk.tensor
            def _(tensor):
                tensor.wait_ge(sG.h, abG)
                tensor.wait_ge(sV.h, ln2V)
                tensor.wait_ge(sR.h, sR.n)
                tensor.wait_ge(sK.h, sK.n)
                tensor.wait_ge(sE.h, sE.n)
                for tb in range(NTB):
                    last = None
                    for dc in range(NCK):
                        last = tensor.matmul(
                            pb[0:64, (4 + tb) * TBW:(5 + tb) * TBW],
                            absb[:, dc * LR:(dc + 1) * LR],
                            xcode[:, dc * T + tb * TBW:dc * T + (tb + 1) * TBW],
                            start=(dc == 0), stop=(dc == NCK - 1))
                    sM.inc(last)
            uM = sM.n

            @blk.scalar
            def _(scalar):
                scalar.wait_ge(sM.h, uM)
                for tb in range(NTB):
                    sE.inc(scalar.copy(out=usb[0:64, tb * TBW:(tb + 1) * TBW],
                                       in_=pb[0:64, (4 + tb) * TBW:(5 + tb) * TBW]))

        for ft in range(NPT):
            @blk.sync
            def _(sync, ft=ft):
                if ft >= 2:
                    sync.wait_ge(sM.h, sM.n)
                sL.inc(sync.dma_start(
                    out=wring[:, (ft % 2) * 1024:(ft % 2 + 1) * 1024].rearrange(
                        "p (c f) -> p c f", c=NCK),
                    in_=Wb[:, ft * 128:(ft + 1) * 128].rearrange(
                        "(c p) f -> p c f", p=128)), 16)
            wL = sL.n

            @blk.tensor
            def _(tensor, ft=ft):
                tensor.wait_ge(sL.h, wL)
                if ft == 0:
                    tensor.wait_ge(sV.h, ln2V)
                    tensor.wait_ge(sE.h, sE.n)
                    tensor.wait_ge(sR.h, sR.n)
                tensor.wait_ge(sK.h, sK.n)
                tensor.wait_ge(sS.h, sS.n)
                tensor.wait_ge(sG.h, sG.n)
                st = 0
                for tb in range(NTB):
                    last = None
                    for dc in range(NCK):
                        last = tensor.matmul(
                            bank(st + tb),
                            wring[:, (ft % 2) * 1024 + dc * 128:(ft % 2) * 1024 + (dc + 1) * 128],
                            xcode[:, dc * T + tb * TBW:dc * T + (tb + 1) * TBW],
                            start=(dc == 0), stop=(dc == NCK - 1 and Aap is None))
                    if Aap is not None:
                        last = tensor.matmul(
                            bank(st + tb), bbsb[:, ft * 128:(ft + 1) * 128],
                            usb[0:64, tb * TBW:(tb + 1) * TBW],
                            start=False, stop=True)
                    sM.inc(last)
            mmM = sM.n

            @blk.vector
            def _(vector, ft=ft):
                vector.wait_ge(sM.h, mmM)
                vector.wait_ge(sL.h, tblL)
                st = 0
                dst = aq if pname == "q" else kshb
                for tb in range(NTB):
                    bk = bank(st + tb)
                    cs_ = slice(tb * TBW, (tb + 1) * TBW)
                    dsl = (dst[:, ft * T + tb * TBW:ft * T + (tb + 1) * TBW]
                           if pname == "q" else dst[:, cs_])
                    if tB is not None:
                        vector.tensor_tensor(out=t2s[0:32, cs_], in0=bk[32:64, :],
                                             in1=tbl2[0:32, cs_], op=ALU.mult)
                        vector.tensor_tensor(out=t2s[32:64, cs_], in0=bk[0:32, :],
                                             in1=tbl2[32:64, cs_], op=ALU.mult)
                        vector.tensor_tensor(out=t2s[64:96, cs_], in0=bk[96:128, :],
                                             in1=tbl2[64:96, cs_], op=ALU.mult)
                        vector.tensor_tensor(out=t2s[96:128, cs_], in0=bk[64:96, :],
                                             in1=tbl2[96:128, cs_], op=ALU.mult)
                        vector.tensor_tensor(out=t3s[:, cs_], in0=bk[:, :],
                                             in1=tbl1[:, cs_], op=ALU.mult)
                        inst = vector.tensor_tensor(out=dsl, in0=t3s[:, cs_],
                                                    in1=t2s[:, cs_], op=ALU.add)
                    else:
                        inst = vector.tensor_tensor(out=dsl, in0=bk[:, :],
                                                    in1=tbl1[:, cs_], op=ALU.mult)
                    if tb == NTB - 1:
                        sK.inc(inst)
            ropeK = sK.n

            if pname == "q":
                @blk.scalar
                def _(scalar, ft=ft):
                    scalar.wait_ge(sK.h, ropeK)
                    sS.inc(scalar.activation(out=sqs[:, :],
                                             in_=aq[:, ft * T:(ft + 1) * T],
                                             func=AF.Square))
                @blk.tensor
                def _(tensor, ft=ft):
                    tensor.wait_ge(sS.h, sS.n)
                    last = None
                    for tb in range(NTB):
                        last = tensor.matmul(
                            pb[0:16, (4 + tb) * TBW:(5 + tb) * TBW],
                            csb["blk16"][:, ft * 16:(ft + 1) * 16],
                            sqs[:, tb * TBW:(tb + 1) * TBW],
                            start=(ft == 0), stop=(ft == NPT - 1))
                    sM.inc(last)
                if ft == NPT - 1:
                    @blk.vector
                    def _(vector):
                        vector.wait_ge(sM.h, sM.n)
                        for tb in range(NTB):
                            vector.reduce_max(out=ch8[0:16, tb:tb + 1],
                                              in_=pb[0:16, (4 + tb) * TBW:(5 + tb) * TBW],
                                              axis=AX.X)
                        vsp(vector)
                        vector.tensor_tensor(out=ch8[0:16, 4:5], in0=ch8[0:16, 0:1],
                                             in1=ch8[0:16, 1:2], op=ALU.max)
                        vector.tensor_tensor(out=ch8[0:16, 5:6], in0=ch8[0:16, 2:3],
                                             in1=ch8[0:16, 3:4], op=ALU.max)
                        vsp(vector)
                        vector.tensor_tensor(out=chead[0:16, 0:1], in0=ch8[0:16, 4:5],
                                             in1=ch8[0:16, 5:6], op=ALU.max)
                        vsp(vector)
                        sK.inc(vector.tensor_copy(out=chead[0:16, 3:4],
                                                  in_=chead[0:16, 0:1]))
            elif pname == "k":
                @blk.vector
                def _(vector, ft=ft):
                    vector.wait_ge(sG.h, sG.n)
                    ksl = lambda j: kshb[:, j:T:4]
                    vector.tensor_tensor(out=st4a[:, :], in0=ksl(0), in1=ksl(1),
                                         op=ALU.add)
                    vector.tensor_tensor(out=st4b[:, :], in0=ksl(2), in1=ksl(3),
                                         op=ALU.add)
                    vector.tensor_tensor(out=kvt[:, :], in0=st4a[:, :], in1=st4b[:, :],
                                         op=ALU.add)
                    vector.tensor_copy(out=PTt[:, 0:T:4], in_=ksl(0))
                    vector.tensor_copy(out=PTt[:, 1:T:4], in_=st4a[:, :])
                    vector.tensor_tensor(out=st4b[:, :], in0=st4a[:, :], in1=ksl(2),
                                         op=ALU.add)
                    vector.tensor_copy(out=PTt[:, 2:T:4], in_=st4b[:, :])
                    vector.tensor_copy(out=PTt[:, 3:T:4], in_=kvt[:, :])
                    vsp(vector)
                    sK.inc(vector.tensor_tensor(out=prodb[:, :],
                                                in0=aq[:, ft * T:(ft + 1) * T],
                                                in1=PTt[:, :], op=ALU.mult))
                kpostK = sK.n

                @blk.gpsimd
                def _(g, ft=ft):
                    g.wait_ge(sK.h, kpostK)
                    sG.inc(g.dma_start(out=kv_in[ft * 128:(ft + 1) * 128, :],
                                       in_=kvt[:, :]), 16)

                @blk.scalar
                def _(scalar, ft=ft):
                    scalar.wait_ge(sK.h, kpostK)
                    sS.inc(scalar.activation(out=kshb[:, :], in_=kshb[:, :],
                                             func=AF.Square))
                @blk.tensor
                def _(tensor, ft=ft):
                    tensor.wait_ge(sK.h, kpostK)
                    tensor.wait_ge(sS.h, sS.n)
                    last = None
                    for tb in range(NTB):
                        tensor.matmul(pb[0:16, (4 + tb) * TBW:(5 + tb) * TBW],
                                      csb["blk16"][:, ft * 16:(ft + 1) * 16],
                                      prodb[:, tb * TBW:(tb + 1) * TBW],
                                      start=(ft == 0), stop=(ft == NPT - 1))
                        last = tensor.matmul(pb[32:48, (4 + tb) * TBW:(5 + tb) * TBW],
                                             csb["blk16"][:, ft * 16:(ft + 1) * 16],
                                             kshb[:, tb * TBW:(tb + 1) * TBW],
                                             start=(ft == 0), stop=(ft == NPT - 1))
                    sM.inc(last)
                if ft == NPT - 1:
                    @blk.scalar
                    def _(scalar):
                        scalar.wait_ge(sM.h, sM.n)
                        for tb in range(NTB):
                            scalar.copy(out=csp[0:16, tb * TBW:(tb + 1) * TBW],
                                        in_=pb[0:16, (4 + tb) * TBW:(5 + tb) * TBW])
                            sS.inc(scalar.copy(out=pool2[0:16, tb * TBW:(tb + 1) * TBW],
                                               in_=pb[32:48, (4 + tb) * TBW:(5 + tb) * TBW]))
                    @blk.vector
                    def _(vector):
                        vector.wait_ge(sS.h, sS.n)
                        vector.tensor_tensor(out=pool2[32:48, 0:TBW],
                                             in0=pool2[0:16, 0:T:4], in1=pool2[0:16, 1:T:4],
                                             op=ALU.add)
                        vector.tensor_tensor(out=PTt[32:48, 0:TBW],
                                             in0=pool2[0:16, 2:T:4], in1=pool2[0:16, 3:T:4],
                                             op=ALU.add)
                        vector.tensor_tensor(out=pool2[32:48, 0:TBW],
                                             in0=pool2[32:48, 0:TBW],
                                             in1=PTt[32:48, 0:TBW], op=ALU.add)
                        vector.reduce_max(out=chead[0:16, 3:4], in_=pool2[32:48, 0:TBW],
                                          axis=AX.X)
                        vsp(vector)
                        sK.inc(vector.tensor_scalar(out=chead[0:16, 1:2],
                                                    in0=chead[0:16, 3:4], scalar1=4.0,
                                                    scalar2=None, op0=ALU.mult))
            else:
                @blk.vector
                def _(vector, ft=ft):
                    vector.wait_ge(sG.h, sG.n)
                    vsl = lambda j: kshb[:, j:T:4]
                    vector.tensor_tensor(out=st4a[:, :], in0=vsl(0), in1=vsl(1),
                                         op=ALU.add)
                    vector.tensor_tensor(out=st4b[:, :], in0=vsl(2), in1=vsl(3),
                                         op=ALU.add)
                    vector.tensor_tensor(out=kvt[:, :], in0=st4a[:, :], in1=st4b[:, :],
                                         op=ALU.add)
                    dco = ft * 1536
                    vector.tensor_tensor(out=Dc[:, dco:dco + TBW], in0=vsl(0),
                                         in1=kvt[:, :], op=ALU.subtract)
                    vector.tensor_tensor(out=Dc[:, dco + TBW:dco + 2 * TBW],
                                         in0=st4a[:, :], in1=kvt[:, :], op=ALU.subtract)
                    vector.tensor_tensor(out=st4b[:, :], in0=st4a[:, :], in1=vsl(2),
                                         op=ALU.add)
                    sK.inc(vector.tensor_tensor(out=Dc[:, dco + 2 * TBW:dco + 3 * TBW],
                                                in0=st4b[:, :], in1=kvt[:, :],
                                                op=ALU.subtract))
                vpostK = sK.n

                @blk.gpsimd
                def _(g, ft=ft):
                    g.wait_ge(sK.h, vpostK)
                    sG.inc(g.dma_start(out=kv_in[(NPT + ft) * 128:(NPT + ft + 1) * 128, :],
                                       in_=kvt[:, :]), 16)
        es_p.close()
        return

    def emit_attention(ln2V, aq, esA, xcode):
        DcBox = [None]
        emit_proj("q", ln2V, aq, esA, DcBox, xcode)
        emit_proj("k", ln2V, aq, esA, DcBox, xcode)
        DcBox[0] = esA.enter_context(nc.sbuf_tensor("Dc", [128, NPT * 1536], BF16, side="right"))
        emit_proj("v", ln2V, aq, esA, DcBox, xcode)
        Dc = DcBox[0]

        @blk.gpsimd
        def _(g):
            g.wait_ge(sG.h, sG.n)
            g.wait_ge(sK.h, sK.n)
            sC.inc(g.collective_compute(
                "AllGather", ALU.bypass,
                replica_groups=[[0, 1], [2, 3], [4, 5], [6, 7]],
                ins=[kv_in], outs=[kv_out]))
        agC = sC.n
        return agC, Dc

    wcastG = emit_wcast()
    emit_wquant(wcastG)
    wcodesG = sG.n
    if _STOP == "wq":
        @blk.gpsimd
        def _(g):
            g.wait_ge(sG.h, wcodesG)
            g.wait_ge(sL.h, sL.n)
            for c in range(NCK):
                sG.inc(g.dma_start(out=yT[c * 128:(c + 1) * 128, :],
                                   in_=Rsb[:, c * T:(c + 1) * T]), 16)
            g.wait_ge(sG.h, sG.n)
        blk = _NullBlk()

    def half_ffn(tag, xcode):
        nonlocal blk
        with (nc.sbuf_tensor("lnfull" + tag, [128, NCK * T], F32) as lnfull,
              nc.sbuf_tensor("lnb" + tag, [128, 2 * T], BF16) as lnb,
              nc.sbuf_tensor("sqb" + tag, [128, 2 * T], BF16) as sqb,
              nc.sbuf_tensor("pool1" + tag, [128, T], F32) as pool1):
            mxV = emit_ln_stats_affine(pool1, lnb, sqb, None, lnfull, xcode, quant=True)
            gW = emit_allreduce_max(mxV)
            q2V = emit_ln_pass2(lnfull, xcode, gW)
            if _STOP == "lnq" + tag:
                @blk.gpsimd
                def _(g):
                    g.wait_ge(sV.h, q2V)
                    for c in range(NCK):
                        sG.inc(g.dma_start(out=yT[c * 128:(c + 1) * 128, :],
                                           in_=Rsb[:, c * T:(c + 1) * T]), 16)
                    g.wait_ge(sG.h, sG.n)
                blk = _NullBlk()
                pass
        with (nc.sbuf_tensor("w1sb" + tag, [128, NCK * FF], BF16) as w1sb,
              nc.sbuf_tensor("hst" + tag, [128, 2 * T], F16) as hst):
            @blk.sync
            def _(sync):
                sync.wait_ge(sV.h, q2V)
            hmaxV, hG = emit_ffn1(w1sb, hst, xcode, wcodesG, q2V)
            if _STOP == "ffn1" + tag:
                @blk.gpsimd
                def _(g):
                    g.wait_ge(sG.h, hG)
                    for c in range(NCK):
                        sG.inc(g.dma_start(out=yT[c * 128:(c + 1) * 128, :],
                                           in_=Rsb[:, c * T:(c + 1) * T]), 16)
                    g.wait_ge(sG.h, sG.n)
                blk = _NullBlk()
        gW2 = emit_allreduce_max(hmaxV)
        if _STOP == "m2" + tag:
            @blk.gpsimd
            def _(g):
                g.wait_ge(sG.h, gW2)
                for c in range(NCK):
                    sG.inc(g.dma_start(out=yT[c * 128:(c + 1) * 128, :],
                                       in_=Rsb[:, c * T:(c + 1) * T]), 16)
            blk = _NullBlk()
        with (nc.sbuf_tensor("w2sb" + tag, [128, NFT * 1024], BF16) as w2sb,
              nc.sbuf_tensor("hring" + tag, [128, 4 * TBW], F16) as hring,
              nc.sbuf_tensor("ust" + tag, [128, 2 * TBW], F32) as ust,
              nc.sbuf_tensor("c2st" + tag, [128, 2 * TBW], BF16) as c2st,
              nc.sbuf_tensor("fst" + tag, [128, 2 * TBW], F32) as fst):
            @blk.sync
            def _(sync):
                sync.wait_ge(sG.h, hG)
            raddV = emit_ffn2(w2sb, hring, ust, c2st, fst, wcodesG, gW2)
        return raddV

    esX1 = ExitStack()
    xcodeA = esX1.enter_context(nc.sbuf_tensor("xcodeA", [128, NCK * T], BF16))
    r1V = half_ffn("a", xcodeA)

    if _STOP == "ffna":
        @blk.gpsimd
        def _(g):
            g.wait_ge(sR.h, r1V)
            for c in range(NCK):
                sG.inc(g.dma_start(out=yT[c * 128:(c + 1) * 128, :],
                                   in_=Rsb[:, c * T:(c + 1) * T]), 16)
            if _DEBUG:
                sG.inc(g.dma_start(out=dbgt["d_a"][:, :], in_=xcodeA[:, :]), 16)
                sG.inc(g.dma_start(out=dbgt["d_b"][0:128, 0:T], in_=pool2[:, :]), 16)
                sG.inc(g.dma_start(out=dbgt["d_b"][0:128, T:T + 16], in_=colsc[:, :]), 16)
                sG.inc(g.dma_start(out=dbgt["d_b"][0:128, T + 16:T + 48], in_=sc1[:, :]), 16)
                sG.inc(g.dma_start(out=dbgt["d_b"][0:128, T + 48:T + 80], in_=bi1[:, :]), 16)
                sG.inc(g.dma_start(out=dbgt["d_b"][0:128, T + 80:T + 112], in_=am1[:, :]), 16)
            g.wait_ge(sG.h, sG.n)
        blk = _NullBlk()

    with (nc.sbuf_tensor("lnbq", [128, 2 * T], BF16) as lnb,
          nc.sbuf_tensor("sqbq", [128, 2 * T], BF16) as sqb,
          nc.sbuf_tensor("lnscrq", [128, TBW], F32) as lnscr,
          nc.sbuf_tensor("pool1q", [128, T], F32) as pool1):
        @blk.scalar
        def _(scalar):
            scalar.wait_ge(sR.h, r1V)
        ln2V = emit_ln_stats_affine(pool1, lnb, sqb, lnscr, None, xcodeA, quant=False)

    if _STOP == "ln2":
        @blk.gpsimd
        def _(g):
            g.wait_ge(sV.h, ln2V)
            for c in range(NCK):
                sG.inc(g.dma_start(out=yT[c * 128:(c + 1) * 128, :],
                                   in_=Rsb[:, c * T:(c + 1) * T]), 16)
            g.wait_ge(sG.h, sG.n)
        blk = _NullBlk()

    esA = ExitStack()
    aq = esA.enter_context(nc.sbuf_tensor("aq", [128, NPT * T], BF16, side="right"))
    agC, Dc = emit_attention(ln2V, aq, esA, xcodeA)

    if _STOP == "proj":
        @blk.gpsimd
        def _(g):
            g.wait_ge(sC.h, agC)
            g.wait_ge(sK.h, sK.n)
            g.wait_ge(sM.h, sM.n)
            for c in range(NCK):
                sG.inc(g.dma_start(out=yT[c * 128:(c + 1) * 128, :],
                                   in_=Rsb[:, c * T:(c + 1) * T]), 16)
            if _DEBUG:
                sG.inc(g.dma_start(out=dbgt["d_a"][:, :], in_=aq[:, :]), 16)
                sG.inc(g.dma_start(out=dbgt["d_b"][:, :], in_=xcodeA[:, :]), 16)
                for i in range(32):
                    sG.inc(g.dma_start(out=dbgt["d_c"][0:128, i * SLO:(i + 1) * SLO],
                                       in_=kv_out[i * 128:(i + 1) * 128, :]), 16)
                for i in range(NCK):
                    sG.inc(g.dma_start(out=dbgt["d_d"][0:128, i * 1024:(i + 1) * 1024],
                                       in_=Wb16["Wk"][i * 128:(i + 1) * 128, :]), 16)
                for i in range(16):
                    sG.inc(g.dma_start(
                        out=dbgt["d_d"][0:128, 8192 + i * SLO:8192 + (i + 1) * SLO],
                        in_=kv_in[i * 128:(i + 1) * 128, :]), 16)
            g.wait_ge(sG.h, sG.n)
        blk = _NullBlk()
    esX1.close()

    for n, sh, dt in (("oneh", [16, NH * 128], BF16), ("mA_past", [128, 512], BF16),
                      ("mA_eq", [128, 512], BF16), ("mB_past", [128, 512], BF16),
                      ("mB_eq", [128, 512], BF16), ("phi2", [64, T], BF16)):
        csb[n] = esA.enter_context(nc.sbuf_tensor("c_" + n, sh, dt, side="right"))
    ecsb = esA.enter_context(nc.sbuf_tensor("ecsb", [16, T], BF16, side="right"))
    ncbank = esA.enter_context(nc.sbuf_tensor("ncbank", [16, NS], BF16, side="right"))
    @blk.sync
    def _(sync):
        sync.wait_ge(sK.h, sK.n)
        sync.wait_ge(sG.h, sG.n)
        sync.wait_ge(sS.h, sS.n)
        for n in ("oneh", "mA_past", "mA_eq", "mB_past", "mB_eq", "phi2"):
            sL.inc(sync.dma_start(out=csb[n][:, :], in_=CN[n][:, :]), 16)
    @blk.gpsimd
    def _(g):
        g.wait_ge(sK.h, sK.n)
        for r in range(16):
            sG.inc(g.dma_start(out=ncbank[r:r + 1, :], in_=CN["slotmask"][:, :]), 16)
    cstL, cstG = sL.n, sG.n
    KTf = esA.enter_context(nc.sbuf_tensor("KTf", [128, NPT * NS], BF16, side="right"))
    Vsm = esA.enter_context(nc.sbuf_tensor("Vsm", [128, NPT * 8 * 130], BF16, side="right"))
    esV = ExitStack()
    VTf = esV.enter_context(nc.sbuf_tensor("VTf", [128, NPT * NS], BF16, side="right"))

    @blk.gpsimd
    def _(g):
        g.wait_ge(sC.h, agC)
        for pt in range(NPT):
            for zz in range(2):
                sG.inc(g.dma_start(
                    out=KTf[:, pt * NS + zz * SLO:pt * NS + (zz + 1) * SLO],
                    in_=kv_out[zz * 2048 + pt * 128:zz * 2048 + (pt + 1) * 128, :]), 16)
                sG.inc(g.dma_start(
                    out=VTf[:, pt * NS + zz * SLO:pt * NS + (zz + 1) * SLO],
                    in_=kv_out[zz * 2048 + 1024 + pt * 128:
                               zz * 2048 + 1024 + (pt + 1) * 128, :]), 16)
    kvG = sG.n

    if _STOP == "ag":
        @blk.gpsimd
        def _(g):
            g.wait_ge(sG.h, kvG)
            for c in range(NCK):
                sG.inc(g.dma_start(out=yT[c * 128:(c + 1) * 128, :],
                                   in_=Rsb[:, c * T:(c + 1) * T]), 16)
            g.wait_ge(sG.h, sG.n)
        blk = _NullBlk()

    with nc.sbuf_tensor("vtstg", [128, 2 * 128], F32) as vtstg:
        @blk.vector
        def _(vector):
            vector.wait_ge(sM.h, sM.n)
            vector.wait_ge(sK.h, sK.n)
            vector.wait_ge(sC.h, agC)
            vector.memset(Vsm[:, 64:NPT * 8 * 130:130], 1.0)
            sV.inc(vector.memset(Vsm[:, 129:NPT * 8 * 130:130], 1.0))
        vsmI = sV.n
        bS, bM, bE = sS.n, sM.n, sE.n

        @blk.scalar
        def _(scalar):
            scalar.wait_ge(sG.h, kvG)
            scalar.wait_ge(sV.h, vsmI)
            for i in range(NPT * 8 + 1):
                if i < NPT * 8:
                    pt, stt = i // 8, i % 8
                    if i >= 2:
                        scalar.wait_ge(sM.h, bM + i - 1)
                    sS.inc(scalar.copy(
                        out=vtstg[:, (i % 2) * 128:(i % 2 + 1) * 128],
                        in_=VTf[:, pt * NS + stt * 128:pt * NS + (stt + 1) * 128]))
                if i >= 1:
                    j = i - 1
                    scalar.wait_ge(sM.h, bM + j + 1)
                    jb = 2 + (j % 2)
                    jpt, jst = j // 8, j % 8
                    base = jpt * 1040 + jst * 130
                    scalar.copy(out=Vsm[:, base:base + 64],
                                in_=pb[:, jb * TBW:jb * TBW + 64])
                    sE.inc(scalar.copy(out=Vsm[:, base + 65:base + 129],
                                       in_=pb[:, jb * TBW + 64:jb * TBW + 128]))

        @blk.tensor
        def _(tensor):
            for i in range(NPT * 8):
                tensor.wait_ge(sS.h, bS + i + 1)
                if i >= 2:
                    tensor.wait_ge(sE.h, bE + i - 1)
                sM.inc(tensor.transpose(
                    pb[:, (2 + (i % 2)) * TBW:(2 + (i % 2)) * TBW + 128],
                    vtstg[:, (i % 2) * 128:(i % 2 + 1) * 128],
                    csb["identf"][:, :]))
    vsmE = sE.n
    esV.close()

    with nc.sbuf_tensor("sqk", [128, NS], BF16) as sqk:
        for pt in range(NPT):
            @blk.scalar
            def _(scalar, pt=pt):
                if pt == 0:
                    scalar.wait_ge(sG.h, kvG)
                scalar.wait_ge(sM.h, sM.n)
                sS.inc(scalar.activation(out=sqk[:, :],
                                         in_=KTf[:, pt * NS:(pt + 1) * NS],
                                         func=AF.Square))
            @blk.tensor
            def _(tensor, pt=pt):
                tensor.wait_ge(sS.h, sS.n)
                if pt == 0:
                    tensor.wait_ge(sE.h, vsmE)
                    tensor.wait_ge(sV.h, sV.n)
                last = None
                for sb_ in range(2):
                    last = tensor.matmul(pb[0:16, sb_ * TBW:(sb_ + 1) * TBW],
                                         csb["blk16"][:, pt * 16:(pt + 1) * 16],
                                         sqk[:, sb_ * TBW:(sb_ + 1) * TBW],
                                         start=(pt == 0), stop=(pt == NPT - 1))
                sM.inc(last)
        @blk.vector
        def _(vector):
            vector.wait_ge(sM.h, sM.n)
            for sb_ in range(2):
                vector.reduce_max(out=ch8[0:16, sb_:sb_ + 1],
                                  in_=pb[0:16, sb_ * TBW:(sb_ + 1) * TBW], axis=AX.X)
            vsp(vector)
            vector.tensor_tensor(out=ch8[0:16, 2:3], in0=ch8[0:16, 0:1],
                                 in1=ch8[0:16, 1:2], op=ALU.max)
            vsp(vector)
            vector.tensor_tensor(out=chead[0:16, 1:2], in0=chead[0:16, 1:2],
                                 in1=ch8[0:16, 2:3], op=ALU.max)
            vsp(vector)
            sV.inc(vector.tensor_copy(out=chead[0:16, 3:4], in_=chead[0:16, 1:2]))

    @blk.scalar
    def _(scalar):
        scalar.wait_ge(sV.h, sV.n)
        scalar.activation(out=chead[:, 2:3], in_=chead[:, 0:1], func=AF.Sqrt,
                          bias=colsc[0:16, 15:16])
        sS.inc(scalar.activation(out=chead[:, 3:4], in_=chead[:, 1:2], func=AF.Sqrt,
                                 bias=colsc[0:16, 15:16]))
    @blk.vector
    def _(vector):
        vector.wait_ge(sS.h, sS.n)
        vector.reduce_max(out=ch8[0:16, 6:7], in_=csp[0:16, :], axis=AX.X)
        vsp(vector)
        vector.tensor_scalar(out=nc16[:, 0:1], in0=ch8[0:16, 6:7],
                             scalar1=-1.0, scalar2=None, op0=ALU.mult)
        vsp(vector)
        vector.tensor_copy(out=ncf[:, 0:1], in_=nc16[:, 0:1])
        vsp(vector)
        vector.wait_ge(sG.h, cstG)
        sV.inc(vector.tensor_scalar(out=ncbank[0:16, :], in0=ncbank[0:16, :],
                                    scalar1=ncf[0:16, 0:1], scalar2=None, op0=ALU.add))
    ncV = sV.n
    @blk.scalar
    def _(scalar):
        scalar.wait_ge(sV.h, ncV)
        sS.inc(scalar.activation(out=ecsb[0:16, :], in_=csp[0:16, :], func=AF.Exp,
                                 bias=ncf[:, 0:1]))
    ecsS = sS.n

    def dump_attn(g):
        if not _DEBUG:
            return
        sG.inc(g.dma_start(out=dbgt["d_a"][:, :], in_=aq[:, :]), 16)
        sG.inc(g.dma_start(out=dbgt["d_b"][0:128, 0:T], in_=pool2[:, :]), 16)
        sG.inc(g.dma_start(out=dbgt["d_b"][0:128, T:T + 16], in_=colsc[:, :]), 16)
        sG.inc(g.dma_start(out=dbgt["d_b"][0:16, T + 16:2 * T + 16], in_=ecsb[0:16, :]), 16)
        sG.inc(g.dma_start(out=dbgt["d_b"][0:16, 2 * T + 16:2 * T + 20], in_=chead[0:16, :]), 16)
        sG.inc(g.dma_start(out=dbgt["d_b"][0:16, 2 * T + 20:2 * T + 20 + NS],
                           in_=ncbank[0:16, :]), 16)
        sG.inc(g.dma_start(out=dbgt["d_b"][0:16, 2 * T + 20 + NS:3 * T + 20 + NS],
                           in_=csp[0:16, :]), 16)
        for i in range(32):
            sG.inc(g.dma_start(out=dbgt["d_c"][0:128, i * SLO:(i + 1) * SLO],
                               in_=kv_out[i * 128:(i + 1) * 128, :]), 16)
        for i in range(NCK):
            sG.inc(g.dma_start(out=dbgt["d_d"][0:128, i * T:(i + 1) * T],
                               in_=o2_d[i * 128:(i + 1) * 128, :]), 16)

    if _STOP == "ecs":
        @blk.gpsimd
        def _(g):
            g.wait_ge(sS.h, ecsS)
            for c in range(NCK):
                sG.inc(g.dma_start(out=yT[c * 128:(c + 1) * 128, :],
                                   in_=Rsb[:, c * T:(c + 1) * T]), 16)
            dump_attn(g)
            g.wait_ge(sG.h, sG.n)
        blk = _NullBlk()

    ETR = 3
    efix = esA.enter_context(nc.sbuf_tensor("efix", [128, TBW], BF16, side="right"))
    o2t = esA.enter_context(nc.sbuf_tensor("o2t", [128, T], BF16, side="right"))
    kst = esA.enter_context(nc.sbuf_tensor("kst", [65, NS], BF16, side="right"))
    ast = esA.enter_context(nc.sbuf_tensor("ast", [65, T], BF16, side="right"))
    et = esA.enter_context(nc.sbuf_tensor("et", [128, ETR * TBW], BF16, side="right"))
    ctb = esA.enter_context(nc.sbuf_tensor("ctb", [64, TBW], BF16, side="right"))
    ev1 = esA.enter_context(nc.sbuf_tensor("ev1", [64, TBW], F32, side="right"))

    @blk.vector
    def _(vector):
        vector.wait_ge(sE.h, vsmE)
        vector.memset(ctb[:, :], 0.0)
        sV.inc(vector.memset(ast[64:65, :], 1.0))
    astI = sV.n

    SCW = [(st, tb) for st in range(NST) for tb in range(max(0, st - 4), NTB)]
    o2spillG = [sG.n]

    for hh in range(NH):
        pt, half = hh // 2, hh % 2
        rowoff = 64 * half
        cnt = {}

        @blk.scalar
        def _(scalar, hh=hh, pt=pt, rowoff=rowoff):
            scalar.wait_ge(sG.h, kvG)
            scalar.wait_ge(sM.h, sM.n)
            scalar.copy(out=kst[0:64, :],
                        in_=KTf[rowoff:rowoff + 64, pt * NS:(pt + 1) * NS])
            sE.inc(scalar.copy(out=ast[0:64, :],
                               in_=aq[rowoff:rowoff + 64, pt * T:(pt + 1) * T]))
        stageE = sE.n

        @blk.sync
        def _(sync, hh=hh):
            sync.wait_ge(sS.h, ecsS)
            sync.wait_ge(sV.h, astI)
            sync.wait_ge(sM.h, sM.n)
            sL.inc(sync.dma_start(out=kst[64:65, :], in_=ncbank[hh:hh + 1, :]), 16)
        stageL = sL.n

        kpred = {}
        kc = sK.n
        for (st, tb) in SCW:
            if st == tb or st == tb + 4:
                kc += 1
                kpred[("fix", st, tb)] = kc
            if st == tb + 4:
                kc += 1
                kpred[("zprep", tb)] = kc

        expbase = sS.n
        rbase = sR.n
        xbase = sX.n

        @blk.tensor
        def _(tensor, hh=hh, pt=pt, half=half, cnt=cnt, kpred=kpred,
              expbase=expbase, rbase=rbase, stageE=stageE, stageL=stageL):
            tensor.wait_ge(sL.h, stageL)
            tensor.wait_ge(sE.h, stageE)
            tensor.wait_ge(sR.h, rbase)
            prev_bce = [None]
            for i, (st, tb) in enumerate(SCW):
                if i >= 2:
                    tensor.wait_ge(sS.h, expbase + i - 1)
                sM.inc(tensor.matmul(
                    bank(4 + i % 2),
                    kst[:, st * 128:(st + 1) * 128],
                    ast[:, tb * TBW:(tb + 1) * TBW],
                    start=True, stop=True))
                cnt[("sc", i)] = sM.n
                if st == tb or st == tb + 4:
                    if prev_bce[0] is not None:
                        pst, ptb = prev_bce[0]
                        tensor.wait_ge(sK.h, kpred[("fix", pst, ptb)])
                        if pst == ptb + 4:
                            tensor.wait_ge(sR.h, rbase + ptb + 1)
                    sM.inc(tensor.matmul(pb[:, 6 * TBW:7 * TBW],
                                         csb["oneh"][:, hh * 128:(hh + 1) * 128],
                                         ecsb[0:16, tb * TBW:(tb + 1) * TBW],
                                         start=True, stop=True))
                    cnt[("bce", i)] = sM.n
                    prev_bce[0] = (st, tb)
                diag = (st == tb or st == tb + 4)
                if diag:
                    tensor.wait_ge(sK.h, kpred[("fix", st, tb)])
                else:
                    tensor.wait_ge(sS.h, expbase + i + 1)
                sX.inc(tensor.matmul(
                    pb[0:65, tb * TBW:(tb + 1) * TBW],
                    Vsm[:, pt * 1040 + st * 130 + half * 65:
                        pt * 1040 + st * 130 + half * 65 + 65],
                    et[:, (i % ETR) * TBW:(i % ETR + 1) * TBW],
                    start=(st == 0), stop=(st == tb + 4)))
                cnt[("av", i)] = sX.n
                if st == tb + 4:
                    tensor.wait_ge(sK.h, kpred[("zprep", tb)])
                    if tb >= 1:
                        tensor.wait_ge(sR.h, rbase + tb)
                    sM.inc(tensor.matmul(pb[0:64, 7 * TBW:7 * TBW + TBW],
                                         csb["onesf"][0:1, 0:64],
                                         pool2[0:1, tb * TBW:(tb + 1) * TBW],
                                         start=True, stop=True))
                    cnt[("zbc", tb)] = sM.n

        @blk.scalar
        def _(scalar, hh=hh, cnt=cnt, xbase=xbase):
            scalar.wait_ge(sX.h, xbase)
            for i, (st, tb) in enumerate(SCW):
                scalar.wait_ge(sM.h, cnt[("sc", i)])
                if i >= ETR:
                    scalar.wait_ge(sX.h, cnt[("av", i - ETR)])
                if tb < st < tb + 4:
                    sS.inc(scalar.activation(
                        out=et[:, (i % ETR) * TBW:(i % ETR + 1) * TBW],
                        in_=bank(4 + i % 2), func=AF.Exp, bias=csb["midb"][:, 0:1]))
                else:
                    sS.inc(scalar.activation(
                        out=et[:, (i % ETR) * TBW:(i % ETR + 1) * TBW],
                        in_=bank(4 + i % 2), func=AF.Exp))

        @blk.vector
        def _(vector, hh=hh, pt=pt, half=half, rowoff=rowoff, cnt=cnt,
              expbase=expbase):
            if half == 0 and pt >= 1:
                vector.wait_ge(sG.h, o2spillG[0])
            for i, (st, tb) in enumerate(SCW):
                if st == tb or st == tb + 4:
                    mp = csb["mA_past"] if st == tb else csb["mB_past"]
                    me = csb["mA_eq"] if st == tb else csb["mB_eq"]
                    vector.wait_ge(sS.h, expbase + i + 1)
                    vector.wait_ge(sM.h, cnt[("bce", i)])
                    esl = et[:, (i % ETR) * TBW:(i % ETR + 1) * TBW]
                    vector.tensor_tensor(out=esl, in0=esl, in1=mp[:, :], op=ALU.mult)
                    vector.tensor_tensor(out=efix[:, :], in0=pb[:, 6 * TBW:7 * TBW],
                                         in1=me[:, :], op=ALU.mult)
                    sK.inc(vector.tensor_tensor(out=esl, in0=esl, in1=efix[:, :],
                                                op=ALU.add))
                if st == tb + 4:
                    vector.wait_ge(sX.h, cnt[("av", i)])
                    vector.tensor_copy(out=pool2[0:1, tb * TBW:(tb + 1) * TBW],
                                       in_=pb[64:65, tb * TBW:(tb + 1) * TBW])
                    vsp(vector, 1)
                    sK.inc(vector.reciprocal(out=pool2[0:1, tb * TBW:(tb + 1) * TBW],
                                             in_=pool2[0:1, tb * TBW:(tb + 1) * TBW]))
                    vector.wait_ge(sM.h, cnt[("zbc", tb)])
                    for jj in range(3):
                        vector.tensor_tensor(
                            out=ctb[:, jj:TBW:4],
                            in0=Dc[rowoff:rowoff + 64,
                                   pt * 1536 + jj * TBW + tb * 128:
                                   pt * 1536 + jj * TBW + (tb + 1) * 128],
                            in1=pb[0:64, 6 * TBW + jj:7 * TBW:4], op=ALU.mult)
                    vsp(vector)
                    vector.tensor_tensor(out=ev1[:, :],
                                         in0=pb[0:64, tb * TBW:(tb + 1) * TBW],
                                         in1=ctb[:, :], op=ALU.add)
                    vector.tensor_tensor(out=ev1[:, :], in0=ev1[:, :],
                                         in1=pb[0:64, 7 * TBW:7 * TBW + TBW],
                                         op=ALU.mult)
                    sR.inc(vector.tensor_tensor(
                        out=o2t[rowoff:rowoff + 64, tb * TBW:(tb + 1) * TBW],
                        in0=ev1[:, :],
                        in1=csb["phi2"][0:64, tb * TBW:(tb + 1) * TBW],
                        op=ALU.mult))
        assert isinstance(blk, _NullBlk) or sK.n == kc, (hh, sK.n, kc)

        if half == 1:
            @blk.gpsimd
            def _(g, pt=pt):
                g.wait_ge(sR.h, sR.n)
                sG.inc(g.dma_start(out=o2_d[pt * 128:(pt + 1) * 128, :],
                                   in_=o2t[:, :]), 16)
            o2spillG[0] = sG.n
    headR = sR.n
    headSpill = sG.n

    if _STOP == "heads":
        @blk.gpsimd
        def _(g):
            g.wait_ge(sR.h, headR)
            g.wait_ge(sG.h, headSpill)
            for c in range(NCK):
                sG.inc(g.dma_start(out=yT[c * 128:(c + 1) * 128, :],
                                   in_=Rsb[:, c * T:(c + 1) * T]), 16)
            dump_attn(g)
            g.wait_ge(sG.h, sG.n)
        blk = _NullBlk()
    esA.close()

    with (nc.sbuf_tensor("o2full", [128, NCK * T], BF16) as o2full,
          nc.sbuf_tensor("wringo", [128, 2 * 1024], BF16) as wringo,
          nc.sbuf_tensor("absbo", [128, NCK * LR], BF16) as absbo,
          nc.sbuf_tensor("bbsbo", [64, 1024], BF16) as bbsbo,
          nc.sbuf_tensor("usbo", [64, T], BF16) as usbo):
        @blk.sync
        def _(sync):
            sync.wait_ge(sG.h, headSpill)
            sync.wait_ge(sR.h, headR)
            for dc in range(NCK):
                sL.inc(sync.dma_start(out=o2full[:, dc * T:(dc + 1) * T],
                                      in_=o2_d[dc * 128:(dc + 1) * 128, :]), 16)
        o2L = sL.n

        @blk.gpsimd
        def _(g):
            g.wait_ge(sR.h, headR)
            for dc in range(NCK):
                sG.inc(g.dma_start(out=absbo[:, dc * LR:(dc + 1) * LR],
                                   in_=AO[dc * 128:(dc + 1) * 128, :]), 16)
            sG.inc(g.dma_start(out=bbsbo[:, :], in_=BO[:, :]), 16)
        oG = sG.n

        @blk.tensor
        def _(tensor):
            tensor.wait_ge(sL.h, o2L)
            tensor.wait_ge(sG.h, oG)
            for tb in range(NTB):
                last = None
                for dc in range(NCK):
                    last = tensor.matmul(
                        pb[0:64, (4 + tb) * TBW:(5 + tb) * TBW],
                        absbo[:, dc * LR:(dc + 1) * LR],
                        o2full[:, dc * T + tb * TBW:dc * T + (tb + 1) * TBW],
                        start=(dc == 0), stop=(dc == NCK - 1))
                sM.inc(last)
        uoM = sM.n

        @blk.scalar
        def _(scalar):
            scalar.wait_ge(sM.h, uoM)
            for tb in range(NTB):
                sE.inc(scalar.copy(out=usbo[0:64, tb * TBW:(tb + 1) * TBW],
                                   in_=pb[0:64, (4 + tb) * TBW:(5 + tb) * TBW]))
        uoE = sE.n

        for ft in range(NCK):
            @blk.sync
            def _(sync, ft=ft):
                if ft >= 2:
                    sync.wait_ge(sM.h, sM.n)
                sL.inc(sync.dma_start(
                    out=wringo[:, (ft % 2) * 1024:(ft % 2 + 1) * 1024].rearrange(
                        "p (c f) -> p c f", c=NCK),
                    in_=Wb16["Wo"][:, ft * 128:(ft + 1) * 128].rearrange(
                        "(c p) f -> p c f", p=128)), 16)
            woL = sL.n

            @blk.tensor
            def _(tensor, ft=ft):
                tensor.wait_ge(sL.h, woL)
                if ft == 0:
                    tensor.wait_ge(sE.h, uoE)
                else:
                    tensor.wait_ge(sR.h, sR.n)
                st = (ft % 2) * 4
                for tb in range(NTB):
                    for dc in range(NCK):
                        tensor.matmul(
                            bank(st + tb),
                            wringo[:, (ft % 2) * 1024 + dc * 128:(ft % 2) * 1024 + (dc + 1) * 128],
                            o2full[:, dc * T + tb * TBW:dc * T + (tb + 1) * TBW],
                            start=(dc == 0), stop=False)
                    last = tensor.matmul(bank(st + tb), bbsbo[:, ft * 128:(ft + 1) * 128],
                                         usbo[0:64, tb * TBW:(tb + 1) * TBW],
                                         start=False, stop=True)
                    if tb == NTB - 1:
                        sM.inc(last)
            myM = sM.n

            @blk.vector
            def _(vector, ft=ft, myM=myM):
                vector.wait_ge(sM.h, myM)
                st = (ft % 2) * 4
                for tb in range(NTB):
                    sR.inc(vector.tensor_tensor(
                        out=Rsb[:, ft * T + tb * TBW:ft * T + (tb + 1) * TBW],
                        in0=Rsb[:, ft * T + tb * TBW:ft * T + (tb + 1) * TBW],
                        in1=bank(st + tb), op=ALU.add))
    yR = sR.n

    if _STOP == "yproj":
        @blk.gpsimd
        def _(g):
            g.wait_ge(sR.h, yR)
            for c in range(NCK):
                sG.inc(g.dma_start(out=yT[c * 128:(c + 1) * 128, :],
                                   in_=Rsb[:, c * T:(c + 1) * T]), 16)
            g.wait_ge(sG.h, sG.n)
        blk = _NullBlk()

    @blk.scalar
    def _(scalar):
        scalar.wait_ge(sR.h, yR)
    @blk.vector
    def _(vector):
        vector.wait_ge(sR.h, yR)
    @blk.sync
    def _(sync):
        sync.wait_ge(sR.h, yR)
    with nc.sbuf_tensor("xcodeB", [128, NCK * T], BF16) as xcodeB:
        r2V = half_ffn("b", xcodeB)

    @blk.gpsimd
    def _(g):
        g.wait_ge(sR.h, r2V)
        for c in range(NCK):
            sG.inc(g.dma_start(out=yT[c * 128:(c + 1) * 128, :],
                               in_=Rsb[:, c * T:(c + 1) * T]), 16)
        g.wait_ge(sG.h, sG.n)

    if _DEBUG:
        pass

    ctx.close()
    return nc


def kernel(L0_attn_hidden_states, L0_attn_Wq, L0_attn_Wk, L0_attn_Wv, L0_attn_Wo,
           L0_attn_Aq, L0_attn_Bq, L0_attn_Av, L0_attn_Bv, L0_attn_Ao, L0_attn_Bo,
           L0_ffn_w1, L0_ffn_b1, L0_ffn_w2, L0_ffn_b2):
    x = np.asarray(L0_attn_hidden_states, dtype=np.float32)
    shared = {
        "Wq": np.asarray(L0_attn_Wq, np.float32), "Wk": np.asarray(L0_attn_Wk, np.float32),
        "Wv": np.asarray(L0_attn_Wv, np.float32), "Wo": np.asarray(L0_attn_Wo, np.float32),
        "Aq": np.asarray(L0_attn_Aq, np.float32), "Av": np.asarray(L0_attn_Av, np.float32),
        "Ao": np.asarray(L0_attn_Ao, np.float32),
        "Bq": np.asarray(L0_attn_Bq, np.float32), "Bv": np.asarray(L0_attn_Bv, np.float32),
        "Bo": np.asarray(L0_attn_Bo, np.float32),
        "w1": np.asarray(L0_ffn_w1, np.float32), "w2": np.asarray(L0_ffn_w2, np.float32),
        "b1l": np.ascontiguousarray(np.asarray(L0_ffn_b1, np.float32).reshape(NFT, 128).T),
        "b2l": np.ascontiguousarray(np.asarray(L0_ffn_b2, np.float32).reshape(NCK, 128).T),
    }
    cc = [_host_consts(z) for z in (0, 1)]
    in_maps = []
    for c in range(8):
        b, z = c // 2, c % 2
        m = dict(shared)
        m.update(cc[z])
        m["xT"] = np.ascontiguousarray(x[b, 2048 * z:2048 * (z + 1), :].T)
        in_maps.append(m)

    nc = build_nc()
    res = run_bass_kernel_spmd(nc, in_maps, core_ids=list(range(8)))

    out = np.empty((B, S, H), dtype=np.float32)
    for c in range(8):
        b, z = c // 2, c % 2
        out[b, 2048 * z:2048 * (z + 1), :] = np.asarray(res.results[c]["yT"]).T
    return out


# version 02 (final)
# speedup vs baseline: 1.0273x; 1.0273x over previous
import numpy as np
import ml_dtypes
from contextlib import ExitStack

import concourse.bass as bass
import concourse.mybir as mybir
from concourse.bass_utils import run_bass_kernel_spmd

F32 = mybir.dt.float32
BF16 = mybir.dt.bfloat16
F16 = mybir.dt.float16
AF = mybir.ActivationFunctionType
ALU = mybir.AluOpType
AX = mybir.AxisListType

B, S, H = 4, 4096, 1024
NH, HD = 16, 64
FF = 4096
LR = 64
T = 2048
NCK = 8
NFT = FF // 128
TBW = 512
NTB = T // TBW
NPT = 8
NST = 8
NS = 1024
SLO = 512
MAGIC = 12582912.0
EPSQ = 1e-8
EPSLN = 1e-5
NEGBIG = -60.0

_DEBUG = False
_FFN2TBS = NTB
_STOP = None

_H4 = np.array([[1, 1, 1, 1], [1, -1, 1, -1], [1, 1, -1, -1], [1, -1, -1, 1]],
               dtype=np.float64)


def _blk16():
    m = np.zeros((128, 128))
    for pt in range(8):
        m[0:64, pt * 16 + 2 * pt] = 1.0
        m[64:128, pt * 16 + 2 * pt + 1] = 1.0
    return m


def _host_consts(z):
    tok0 = 2048 * z
    tg = np.arange(tok0, tok0 + T, dtype=np.float64)
    inv_freq = 10000.0 ** (-(2.0 * np.arange(32, dtype=np.float64) / HD))
    emb = tg[:, None] * inv_freq[None, :]
    emb = np.concatenate([emb, emb], axis=1)
    cos, sin = np.cos(emb), np.sin(emb)
    jj = np.arange(HD)
    phi = _H4[(tg.astype(np.int64) % 4)][:, (jj % 4)]
    sgn = np.where(jj < 32, -1.0, 1.0)[None, :]

    def fm(x):
        a = np.ascontiguousarray(x.T)
        return np.concatenate([a, a], axis=0)

    bf = ml_dtypes.bfloat16
    ds = np.arange(128)[:, None]
    dt = np.arange(512)[None, :]
    real_past = (ds < (dt // 4)).astype(np.float64)
    real_eq = (ds == (dt // 4)).astype(np.float64)
    noop_past = np.ones((128, 512))
    noop_eq = np.zeros((128, 512))
    if z == 0:
        mA_past, mA_eq, mB_past, mB_eq = real_past, real_eq, noop_past, noop_eq
        slotmask = np.concatenate([np.zeros(SLO), np.full(NS - SLO, NEGBIG)])
    else:
        mA_past, mA_eq, mB_past, mB_eq = noop_past, noop_eq, real_past, real_eq
        slotmask = np.zeros(NS)
    selpair = np.zeros((2, 128))
    selpair[0, :64] = 1.0
    selpair[1, 64:] = 1.0
    return {
        "qC1": fm(cos * phi / 8.0).astype(bf),
        "qC2": fm(sin * sgn * phi / 8.0).astype(bf),
        "kC1": fm(cos * phi / 2.0).astype(bf),
        "kC2": fm(sin * sgn * phi / 2.0).astype(bf),
        "vphi": fm(phi / 2.0).astype(bf),
        "phi2": np.ascontiguousarray(fm(phi * 2.0)[:64]).astype(bf),
        "mA_past": mA_past.astype(bf), "mA_eq": mA_eq.astype(bf),
        "mB_past": mB_past.astype(bf), "mB_eq": mB_eq.astype(bf),
        "slotmask": slotmask.reshape(1, NS).astype(np.float32),
        "identb": np.eye(128, dtype=bf),
        "identf": np.eye(128, dtype=np.float32),
        "blk2": np.stack([np.repeat([1.0, 0.0], 64), np.repeat([0.0, 1.0], 64)],
                         axis=1).astype(bf),
        "blk16": _blk16().astype(bf),
        "selp": np.ascontiguousarray(selpair.astype(np.float32)),
        "onesf": np.ones((128, 128), dtype=np.float32),
        "midb": np.full((128, 1), (-100.0 if z == 0 else 0.0), dtype=np.float32),
        "oneh": np.ascontiguousarray(np.repeat(np.eye(16), 128, axis=1)).astype(bf),
        "onescol": np.ones((128, 1), dtype=bf),
    }


class _NullBlk:
    def __getattr__(self, name):
        def deco(f):
            return None
        return deco


class _Sem:
    def __init__(self, ctx, nc, name):
        self.h = ctx.enter_context(nc.semaphore(name))
        self.n = 0

    def inc(self, inst, k=1):
        inst.then_inc(self.h, k)
        self.n += k
        return inst


def build_nc():
    nc = bass.Bass("TRN2", target_bir_lowering=False, detect_race_conditions=False)
    ctx = ExitStack()

    def din(name, shape, dt=F32):
        return nc.dram_tensor(name, shape, dt, kind="ExternalInput").ap()

    xT = din("xT", [H, T])
    WQ, WK, WV, WO = (din(n, [H, H]) for n in ("Wq", "Wk", "Wv", "Wo"))
    AQ, AV_, AO = (din(n, [H, LR]) for n in ("Aq", "Av", "Ao"))
    BQ, BV_, BO = (din(n, [LR, H]) for n in ("Bq", "Bv", "Bo"))
    w1 = din("w1", [FF, H])
    w2 = din("w2", [H, FF])
    b1l = din("b1l", [128, NFT])
    b2l = din("b2l", [128, NCK])
    CN = {}
    for n, sh, dt in (("qC1", [128, T], BF16), ("qC2", [128, T], BF16),
                      ("kC1", [128, T], BF16), ("kC2", [128, T], BF16),
                      ("vphi", [128, T], BF16), ("phi2", [64, T], BF16),
                      ("mA_past", [128, 512], BF16), ("mA_eq", [128, 512], BF16),
                      ("mB_past", [128, 512], BF16), ("mB_eq", [128, 512], BF16),
                      ("slotmask", [1, NS], F32), ("identb", [128, 128], BF16),
                      ("identf", [128, 128], F32), ("blk2", [128, 2], BF16),
                      ("blk16", [128, 128], BF16),
                      ("selp", [2, 128], F32), ("oneh", [16, NH * 128], BF16),
                      ("midb", [128, 1], F32),
                      ("onesf", [128, 128], F32), ("onescol", [128, 1], BF16)):
        CN[n] = din(n, sh, dt)

    yT = nc.dram_tensor("yT", [H, T], F32, kind="ExternalOutput").ap()

    okind = {}
    w1c_d = nc.dram_tensor("w1c_d", [FF, H], BF16, **okind).ap()
    w2c_d = nc.dram_tensor("w2c_d", [H, FF], BF16, **okind).ap()
    h_d = nc.dram_tensor("h_d", [NTB, FF, TBW], F16, **okind).ap()
    mx_in = nc.dram_tensor("mx_in", [1, 16], F32).ap()
    mx_out = nc.dram_tensor("mx_out", [1, 16], F32).ap()
    kv_in = nc.dram_tensor("kv_in", [2 * NPT * 128, SLO], BF16).ap()
    kv_out = nc.dram_tensor("kv_out", [2 * 2 * NPT * 128, SLO], BF16, **okind).ap()
    dbgt = {}
    if _DEBUG:
        for n, sh, dt in (("d_a", [128, NCK * T], F32), ("d_b", [128, NCK * T], F32),
                          ("d_c", [128, NCK * T], F32), ("d_d", [128, NCK * T], F32)):
            dbgt[n] = nc.dram_tensor(n, sh, dt, kind="ExternalOutput").ap()

    sbp = lambda name, sh, dt: ctx.enter_context(nc.sbuf_tensor(name, sh, dt))

    Rsb = sbp("Rsb", [128, NCK * T], F32)
    pool2 = sbp("pool2", [128, T], F32)
    csp = sbp("csp", [16, T], F32)
    colsc = sbp("colsc", [128, 16], F32)
    b1sb = sbp("b1sb", [128, NFT], F32)
    b2sb = sbp("b2sb", [128, NCK], F32)
    am1 = sbp("am1", [128, NFT], F32)
    am2 = sbp("am2", [128, NCK], F32)
    sc1 = sbp("sc1", [128, NFT], F32)
    bi1 = sbp("bi1", [128, NFT], F32)
    iv1 = sbp("iv1", [128, NFT], F32)
    sc2 = sbp("sc2", [128, NCK], F32)
    bi2 = sbp("bi2", [128, NCK], F32)
    iv2 = sbp("iv2", [128, NCK], F32)
    qscr = sbp("qscr", [128, NFT], F32)
    csb = {}
    for n in ("identb", "identf", "blk2", "blk16", "selp", "onesf", "onescol",
              "midb"):
        csb[n] = sbp("c_" + n, list(CN[n].shape), CN[n].dtype)
    chead = sbp("chead", [16, 4], F32)
    nc16 = sbp("nc16", [16, 1], BF16)
    ncf = sbp("ncf", [16, 1], F32)
    ch8 = sbp("ch8", [16, 8], F32)
    vnop = sbp("vnop", [128, 2], F32)

    def vsp(vector, n=2):
        for _ in range(n):
            vector.tensor_copy(out=vnop[:, 0:1], in_=vnop[:, 1:2])



    pb = ctx.enter_context(nc.psum_tensor("pb", [128, 4096], F32))

    def bank(i, w=TBW):
        return pb[:, i * TBW:i * TBW + w]

    sL = _Sem(ctx, nc, "sL")
    sG = _Sem(ctx, nc, "sG")
    sS = _Sem(ctx, nc, "sS")
    sV = _Sem(ctx, nc, "sV")
    sT = _Sem(ctx, nc, "sT")
    sC = _Sem(ctx, nc, "sC")
    sM = _Sem(ctx, nc, "sM")
    sE = _Sem(ctx, nc, "sE")
    sX = _Sem(ctx, nc, "sX")
    sR = _Sem(ctx, nc, "sR")
    sK = _Sem(ctx, nc, "sK")

    blk = ctx.enter_context(nc.Block())

    @blk.sync
    def _(sync):
        for c in range(NCK):
            sL.inc(sync.dma_start(out=Rsb[:, c * T:(c + 1) * T],
                                  in_=xT[c * 128:(c + 1) * 128, :]), 16)
        for n in csb:
            sL.inc(sync.dma_start(out=csb[n][:, :], in_=CN[n][:, :]), 16)

        sL.inc(sync.dma_start(out=b1sb[:, :], in_=b1l[:, :]), 16)
        sL.inc(sync.dma_start(out=b2sb[:, :], in_=b2l[:, :]), 16)
    p0_loads = sL.n

    @blk.vector
    def _(vector):
        vector.memset(pool2[0:1, :], 0.0)
        vector.memset(pool2[64:65, 0:16], 0.0)
        vector.memset(colsc[:, :], 0.0)
        vector.memset(colsc[:, 8:9], MAGIC)
        sV.inc(vector.memset(colsc[:, 9:10], EPSLN))
    p0_done_v = sV.n

    def emit_wquant(baseG_wcast):
        jobs = []
        for ft in range(NFT):
            jobs.append((w1[ft * 128:(ft + 1) * 128, :], am1[:, ft:ft + 1], True,
                         w1c_d[ft * 128:(ft + 1) * 128, :], iv1[:, ft:ft + 1]))
        for dc in range(NCK):
            for sub in range(4):
                jobs.append((w2[dc * 128:(dc + 1) * 128, sub * 1024:(sub + 1) * 1024],
                             am2[:, dc:dc + 1], sub == 0,
                             w2c_d[dc * 128:(dc + 1) * 128, sub * 1024:(sub + 1) * 1024],
                             iv2[:, dc:dc + 1]))
        nj = len(jobs)
        with (nc.sbuf_tensor("wst", [128, 2 * 1024], F32) as wst,
              nc.sbuf_tensor("cst", [128, 2 * 1024], BF16) as cst,
              nc.sbuf_tensor("ust0", [128, 2 * 1024], F32) as ust0):
            baseL, baseV = sL.n, sV.n
            @blk.sync
            def _(sync):
                sync.wait_ge(sG.h, baseG_wcast)
                for i, (src, amc, first, dst, ivc) in enumerate(jobs):
                    if i >= 2:
                        sync.wait_ge(sV.h, baseV + i - 1)
                    sL.inc(sync.dma_start(out=wst[:, (i % 2) * 1024:(i % 2 + 1) * 1024],
                                          in_=src), 16)

            @blk.vector
            def _(vector):
                for i, (src, amc, first, dst, ivc) in enumerate(jobs):
                    vector.wait_ge(sL.h, baseL + (i + 1) * 16)
                    wsl = wst[:, (i % 2) * 1024:(i % 2 + 1) * 1024]
                    if first:
                        sV.inc(vector.reduce_max(out=amc, in_=wsl, axis=AX.X,
                                                 apply_absolute_value=True))
                    else:
                        vector.reduce_max(out=colsc[:, 0:1], in_=wsl, axis=AX.X,
                                          apply_absolute_value=True)
                        vsp(vector)
                        sV.inc(vector.tensor_tensor(out=amc, in0=amc,
                                                    in1=colsc[:, 0:1], op=ALU.max))
                vector.tensor_single_scalar(out=am1[:, :], in_=am1[:, :],
                                            scalar=EPSQ, op=ALU.max)
                vector.tensor_single_scalar(out=am2[:, :], in_=am2[:, :],
                                            scalar=EPSQ, op=ALU.max)
                vsp(vector)
                vector.reciprocal(out=iv1[:, :], in_=am1[:, :])
                vector.reciprocal(out=iv2[:, :], in_=am2[:, :])
                vsp(vector)
                vector.tensor_single_scalar(out=iv1[:, :], in_=iv1[:, :],
                                            scalar=127.0, op=ALU.mult)
                sV.inc(vector.tensor_single_scalar(out=iv2[:, :], in_=iv2[:, :],
                                                   scalar=127.0, op=ALU.mult))
            ivdone = sV.n

            baseL2, baseS2, baseV2, baseG2 = sL.n, sS.n, sV.n, sG.n

            @blk.sync
            def _(sync):
                sync.wait_ge(sV.h, ivdone)
                for i, (src, amc, first, dst, ivc) in enumerate(jobs):
                    if i >= 2:
                        sync.wait_ge(sV.h, baseV2 + i - 1)
                    sL.inc(sync.dma_start(out=wst[:, (i % 2) * 1024:(i % 2 + 1) * 1024],
                                          in_=src), 16)

            @blk.scalar
            def _(scalar):
                for i, (src, amc, first, dst, ivc) in enumerate(jobs):
                    scalar.wait_ge(sL.h, baseL2 + (i + 1) * 16)
                    if i >= 2:
                        scalar.wait_ge(sV.h, baseV2 + i - 1)
                    sS.inc(scalar.activation(
                        out=ust0[:, (i % 2) * 1024:(i % 2 + 1) * 1024],
                        in_=wst[:, (i % 2) * 1024:(i % 2 + 1) * 1024],
                        func=AF.Identity, scale=ivc, bias=colsc[:, 8:9]))

            @blk.vector
            def _(vector):
                for i, (src, amc, first, dst, ivc) in enumerate(jobs):
                    vector.wait_ge(sS.h, baseS2 + i + 1)
                    if i >= 2:
                        vector.wait_ge(sG.h, baseG2 + 16 * (i - 1))
                    sV.inc(vector.tensor_single_scalar(
                        out=cst[:, (i % 2) * 1024:(i % 2 + 1) * 1024],
                        in_=ust0[:, (i % 2) * 1024:(i % 2 + 1) * 1024],
                        scalar=MAGIC, op=ALU.subtract))

            @blk.gpsimd
            def _(g):
                for i, (src, amc, first, dst, ivc) in enumerate(jobs):
                    g.wait_ge(sV.h, baseV2 + i + 1)
                    sG.inc(g.dma_start(out=dst,
                                       in_=cst[:, (i % 2) * 1024:(i % 2 + 1) * 1024]), 16)
        return sG.n, sV.n

    def emit_allreduce_max(vwait):
        @blk.gpsimd
        def _(g):
            g.wait_ge(sV.h, vwait)
            sG.inc(g.dma_start(out=mx_in[:, :], in_=pool2[64:65, 0:16]), 16)
            g.wait_ge(sG.h, sG.n)
            sC.inc(g.collective_compute(
                "AllReduce", ALU.max, replica_groups=[list(range(8))],
                ins=[mx_in], outs=[mx_out]))
            g.wait_ge(sC.h, sC.n)
            sG.inc(g.dma_start(out=pool2[64:65, 1:2], in_=mx_out[0:1, 0:1]), 16)
        return sG.n

    def emit_ln_stats_affine(pool1, lnb, sqb, lnscr, lnfull, xcode, quant):
        baseS = sS.n
        baseT = sT.n
        gateL, gateG, gateV = sL.n, sG.n, sV.n
        @blk.scalar
        def _(scalar):
            scalar.wait_ge(sL.h, gateL)
            scalar.wait_ge(sG.h, gateG)
            scalar.wait_ge(sV.h, gateV)
            for dc in range(NCK):
                if dc >= 2:
                    scalar.wait_ge(sT.h, baseT + dc - 1)
                scalar.copy(out=lnb[:, (dc % 2) * T:(dc % 2 + 1) * T],
                            in_=Rsb[:, dc * T:(dc + 1) * T])
                sS.inc(scalar.activation(out=sqb[:, (dc % 2) * T:(dc % 2 + 1) * T],
                                         in_=lnb[:, (dc % 2) * T:(dc % 2 + 1) * T],
                                         func=AF.Square))
        @blk.tensor
        def _(tensor):
            for dc in range(NCK):
                tensor.wait_ge(sS.h, baseS + dc + 1)
                for tb in range(NTB):
                    tensor.matmul(pb[0:1, tb * TBW:(tb + 1) * TBW],
                                  csb["onescol"][:, :],
                                  lnb[:, (dc % 2) * T + tb * TBW:(dc % 2) * T + (tb + 1) * TBW],
                                  start=(dc == 0), stop=(dc == NCK - 1))
                for tb in range(NTB):
                    inst = tensor.matmul(pb[0:1, (4 + tb) * TBW:(5 + tb) * TBW],
                                         csb["onescol"][:, :],
                                         sqb[:, (dc % 2) * T + tb * TBW:(dc % 2) * T + (tb + 1) * TBW],
                                         start=(dc == 0), stop=(dc == NCK - 1))
                    if tb == NTB - 1:
                        sT.inc(inst)
        statsT = sT.n

        baseV = sV.n
        @blk.vector
        def _(vector):
            vector.wait_ge(sT.h, statsT)
            for tb in range(NTB):
                vector.tensor_scalar(out=csp[0:1, tb * TBW:(tb + 1) * TBW],
                                     in0=pb[0:1, tb * TBW:(tb + 1) * TBW],
                                     scalar1=1.0 / 1024.0, scalar2=None, op0=ALU.mult)
                vector.tensor_scalar(out=pool1[96:97, tb * TBW:(tb + 1) * TBW],
                                     in0=pb[0:1, (4 + tb) * TBW:(5 + tb) * TBW],
                                     scalar1=1.0 / 1024.0, scalar2=None, op0=ALU.mult)
            vector.tensor_tensor(out=pool2[96:97, :], in0=csp[0:1, :].copy() if False
                                 else pool2[96:97, :], in1=pool2[96:97, :], op=ALU.bypass)                 if False else None
            vector.tensor_copy(out=pool2[96:97, :], in_=csp[0:1, :])
            vector.tensor_tensor(out=pool2[96:97, :], in0=pool2[96:97, :],
                                 in1=pool2[96:97, :], op=ALU.mult)
            sV.inc(vector.tensor_tensor(out=pool1[96:97, :], in0=pool1[96:97, :],
                                        in1=pool2[96:97, :], op=ALU.subtract))
        varV = sV.n

        @blk.scalar
        def _(scalar):
            scalar.wait_ge(sV.h, varV)
            sS.inc(scalar.activation(out=pool2[96:97, :], in_=pool1[96:97, :],
                                     func=AF.Sqrt, bias=colsc[96:97, 9:10]))
        sdS = sS.n

        @blk.vector
        def _(vector):
            vector.wait_ge(sS.h, sdS)
            vector.reciprocal(out=pool1[0:1, :], in_=pool2[96:97, :])
            vector.tensor_tensor(out=pool2[0:1, :], in0=csp[0:1, :],
                                 in1=pool1[0:1, :], op=ALU.mult)
            sV.inc(vector.tensor_scalar(out=pool2[0:1, :],
                                        in0=pool2[0:1, :], scalar1=-1.0,
                                        scalar2=None, op0=ALU.mult))
        abV = sV.n

        @blk.tensor
        def _(tensor):
            tensor.wait_ge(sV.h, abV)
            for tb in range(NTB):
                tensor.matmul(bank(tb), csb["onesf"][0:1, :],
                              pool1[0:1, tb * TBW:(tb + 1) * TBW],
                              start=True, stop=True)
                inst = tensor.matmul(bank(4 + tb), csb["onesf"][0:1, :],
                                     pool2[0:1, tb * TBW:(tb + 1) * TBW],
                                     start=True, stop=True)
                if tb == NTB - 1:
                    sT.inc(inst)
        bcT = sT.n

        @blk.vector
        def _(vector):
            vector.wait_ge(sT.h, bcT)
            for dc in range(NCK):
                for tb in range(NTB):
                    dst = (lnfull[:, dc * T + tb * TBW:dc * T + (tb + 1) * TBW]
                           if quant else lnscr[:, 0:TBW])
                    vector.tensor_tensor(out=dst,
                                         in0=Rsb[:, dc * T + tb * TBW:dc * T + (tb + 1) * TBW],
                                         in1=bank(tb), op=ALU.mult)
                    out2 = (dst if quant else
                            xcode[:, dc * T + tb * TBW:dc * T + (tb + 1) * TBW])
                    vector.tensor_tensor(out=out2, in0=dst, in1=bank(4 + tb),
                                         op=ALU.add)
                if quant:
                    vector.reduce_max(out=colsc[:, 0:1],
                                      in_=lnfull[:, dc * T:(dc + 1) * T], axis=AX.X)
                    vsp(vector)
                    if dc == 0:
                        vector.tensor_copy(out=colsc[:, 2:3], in_=colsc[:, 0:1])
                    else:
                        vector.tensor_tensor(out=colsc[:, 2:3], in0=colsc[:, 2:3],
                                             in1=colsc[:, 0:1], op=ALU.max)
            if quant:
                vsp(vector)
            sV.inc(vector.engine_nop() if False else
                   vector.tensor_copy(out=colsc[:, 1:2], in_=colsc[:, 2:3]))
        affV = sV.n

        if not quant:
            return affV

        @blk.tensor
        def _(tensor):
            tensor.wait_ge(sV.h, affV)
            sT.inc(tensor.transpose(pb[0:1, 0:128], colsc[:, 1:2], csb["identf"][:, :]))
        trT = sT.n

        @blk.vector
        def _(vector):
            vector.wait_ge(sT.h, trT)
            sV.inc(vector.reduce_max(out=pool2[64:65, 0:1], in_=pb[0:1, 0:128], axis=AX.X))
        return sV.n

    def emit_ln_pass2(lnfull, xcode, gwait):
        @blk.vector
        def _(vector):
            vector.wait_ge(sG.h, gwait)
            vector.tensor_scalar(out=pool2[64:65, 2:3], in0=pool2[64:65, 1:2], scalar1=EPSQ,
                                 scalar2=1.0 / 255.0, op0=ALU.max, op1=ALU.mult)
            vsp(vector)
            sV.inc(vector.reciprocal(out=pool2[64:65, 3:4], in_=pool2[64:65, 2:3]))
        invV = sV.n

        @blk.tensor
        def _(tensor):
            tensor.wait_ge(sV.h, invV)
            sT.inc(tensor.matmul(pb[:, 0:1], csb["onesf"][64:65, :], pool2[64:65, 3:4],
                                 start=True, stop=True))
        bcT = sT.n

        @blk.vector
        def _(vector):
            vector.wait_ge(sT.h, bcT)
            vector.tensor_copy(out=colsc[:, 3:4], in_=pb[:, 0:1])
            vsp(vector)
            for dc in range(NCK):
                vector.tensor_scalar(out=lnfull[:, dc * T:(dc + 1) * T],
                                     in0=lnfull[:, dc * T:(dc + 1) * T],
                                     scalar1=colsc[:, 3:4], scalar2=MAGIC,
                                     op0=ALU.mult, op1=ALU.add)
                sV.inc(vector.tensor_scalar(out=xcode[:, dc * T:(dc + 1) * T],
                                            in0=lnfull[:, dc * T:(dc + 1) * T],
                                            scalar1=MAGIC, scalar2=0.0,
                                            op0=ALU.subtract, op1=ALU.max))
        return sV.n

    def emit_ffn1(w1sb, hst, xcode, gwait_codes, gwait_scales):
        baseVp = sV.n
        @blk.vector
        def _(vector):
            vector.wait_ge(sV.h, gwait_scales)
            sV.inc(vector.tensor_scalar(out=pool2[64:65, 5:6], in0=pool2[64:65, 2:3],
                                        scalar1=1.0 / 127.0, scalar2=None, op0=ALU.mult))
        s1V = sV.n

        @blk.tensor
        def _(tensor):
            tensor.wait_ge(sV.h, s1V)
            sT.inc(tensor.matmul(pb[:, 0:1], csb["onesf"][64:65, :], pool2[64:65, 5:6],
                                 start=True, stop=True))
        bcT = sT.n

        @blk.vector
        def _(vector):
            vector.wait_ge(sT.h, bcT)
            vector.tensor_copy(out=colsc[:, 5:6], in_=pb[:, 0:1])
            vsp(vector)
            vector.tensor_scalar(out=sc1[:, :], in0=am1[:, :], scalar1=colsc[:, 5:6],
                                 scalar2=None, op0=ALU.mult)
            vsp(vector)
            vector.reciprocal(out=qscr[:, :], in_=sc1[:, :])
            vsp(vector)
            vector.tensor_tensor(out=qscr[:, :], in0=b1sb[:, :], in1=qscr[:, :],
                                 op=ALU.mult)
            vsp(vector)
            vector.tensor_scalar(out=qscr[:, :], in0=qscr[:, :], scalar1=MAGIC,
                                 scalar2=MAGIC, op0=ALU.add, op1=ALU.subtract)
            vsp(vector)
            sV.inc(vector.tensor_tensor(out=bi1[:, :], in0=qscr[:, :], in1=sc1[:, :],
                                        op=ALU.mult))
        prepV = sV.n

        baseLw = sL.n
        @blk.sync
        def _(sync):
            sync.wait_ge(sG.h, gwait_codes)
            for dc in range(NCK):
                sL.inc(sync.dma_start_transpose(w1sb[:, dc * FF:(dc + 1) * FF],
                                                w1c_d[:, dc * 128:(dc + 1) * 128]), 16)
        wldL = sL.n

        baseM, baseE, baseG, baseV2 = sM.n, sE.n, sG.n, sV.n
        @blk.tensor
        def _(tensor):
            tensor.wait_ge(sL.h, wldL)
            tensor.wait_ge(sV.h, prepV)
            for ft in range(NFT):
                if ft >= 2:
                    tensor.wait_ge(sE.h, baseE + 4 * (ft - 1))
                    tensor.wait_ge(sV.h, baseV2 + 4 * (ft - 1))
                st = (ft % 2) * 4
                for dc in range(NCK):
                    lhsT = w1sb[:, dc * FF + ft * 128:dc * FF + (ft + 1) * 128]
                    for tb in range(NTB):
                        inst = tensor.matmul(
                            bank(st + tb), lhsT,
                            xcode[:, dc * T + tb * TBW:dc * T + (tb + 1) * TBW],
                            start=(dc == 0), stop=(dc == NCK - 1))
                        if dc == NCK - 1 and tb == NTB - 1:
                            sM.inc(inst)

        @blk.scalar
        def _(scalar):
            for ft in range(NFT):
                scalar.wait_ge(sM.h, baseM + ft + 1)
                if ft >= 2:
                    scalar.wait_ge(sG.h, baseG + 16 * (ft - 1))
                    scalar.wait_ge(sV.h, baseV2 + 4 * (ft - 1))
                st = (ft % 2) * 4
                for tb in range(NTB):
                    sE.inc(scalar.activation(
                        out=hst[:, (ft % 2) * T + tb * TBW:(ft % 2) * T + (tb + 1) * TBW],
                        in_=bank(st + tb), func=AF.Relu,
                        scale=sc1[:, ft:ft + 1], bias=bi1[:, ft:ft + 1]))

        @blk.vector
        def _(vector):
            for ft in range(NFT):
                for tb in range(NTB):
                    vector.wait_ge(sE.h, baseE + 4 * ft + tb + 1)
                    vector.reduce_max(
                        out=colsc[:, 0:1],
                        in_=hst[:, (ft % 2) * T + tb * TBW:(ft % 2) * T + (tb + 1) * TBW],
                        axis=AX.X)
                    vsp(vector)
                    if ft == 0 and tb == 0:
                        sV.inc(vector.tensor_copy(out=colsc[:, 6:7], in_=colsc[:, 0:1]))
                    else:
                        sV.inc(vector.tensor_tensor(out=colsc[:, 6:7], in0=colsc[:, 6:7],
                                                    in1=colsc[:, 0:1], op=ALU.max))
        hmaxV = sV.n

        @blk.gpsimd
        def _(g):
            for ft in range(NFT):
                g.wait_ge(sE.h, baseE + 4 * (ft + 1))
                sG.inc(g.dma_start(
                    out=h_d[:, ft * 128:(ft + 1) * 128, :].rearrange("tb p t -> p tb t"),
                    in_=hst[:, (ft % 2) * T:(ft % 2 + 1) * T].rearrange(
                        "p (tb t) -> p tb t", tb=NTB)), 16)
        hstG = sG.n

        @blk.tensor
        def _(tensor):
            tensor.wait_ge(sV.h, hmaxV)
            sT.inc(tensor.transpose(pb[0:1, 0:128], colsc[:, 6:7], csb["identf"][:, :]))
        trT = sT.n

        @blk.vector
        def _(vector):
            vector.wait_ge(sT.h, trT)
            sV.inc(vector.reduce_max(out=pool2[64:65, 0:1], in_=pb[0:1, 0:128], axis=AX.X))
        return sV.n, hstG

    def emit_ffn2(w2sb, hbuf, ust, c2st, fst, gwait_codes, gwait_m2):
        @blk.vector
        def _(vector):
            vector.wait_ge(sG.h, gwait_m2)
            vector.tensor_scalar(out=pool2[64:65, 6:7], in0=pool2[64:65, 1:2], scalar1=EPSQ,
                                 scalar2=1.0 / 255.0, op0=ALU.max, op1=ALU.mult)
            vsp(vector)
            vector.reciprocal(out=pool2[64:65, 7:8], in_=pool2[64:65, 6:7])
            vector.tensor_scalar(out=pool2[64:65, 8:9], in0=pool2[64:65, 6:7],
                                 scalar1=0.5 / 127.0, scalar2=None, op0=ALU.mult)
            sV.inc(vector.tensor_scalar(out=pool2[64:65, 9:10], in0=pool2[64:65, 6:7],
                                        scalar1=1.0 / 127.0, scalar2=None, op0=ALU.mult))
        sclV = sV.n

        @blk.tensor
        def _(tensor):
            tensor.wait_ge(sV.h, sclV)
            tensor.matmul(pb[:, 0:1], csb["onesf"][64:65, :], pool2[64:65, 7:8],
                          start=True, stop=True)
            tensor.matmul(pb[:, 1:2], csb["onesf"][64:65, :], pool2[64:65, 8:9],
                          start=True, stop=True)
            sT.inc(tensor.matmul(pb[:, 2:3], csb["onesf"][64:65, :], pool2[64:65, 9:10],
                                 start=True, stop=True))
        bc2T = sT.n

        @blk.vector
        def _(vector):
            vector.wait_ge(sT.h, bc2T)
            vector.tensor_copy(out=colsc[:, 10:11], in_=pb[:, 1:2])
            vector.tensor_copy(out=colsc[:, 11:12], in_=pb[:, 2:3])
            vector.tensor_copy(out=colsc[:, 7:8], in_=pb[:, 0:1])
            vector.tensor_scalar(out=sc2[:, :], in0=am2[:, :], scalar1=colsc[:, 10:11],
                                 scalar2=None, op0=ALU.mult)
            vector.tensor_scalar(out=iv2[:, 0:NCK], in0=am2[:, :], scalar1=colsc[:, 11:12],
                                 scalar2=None, op0=ALU.mult)
            vsp(vector)
            vector.reciprocal(out=iv2[:, 0:NCK], in_=iv2[:, 0:NCK])
            vsp(vector)
            vector.tensor_tensor(out=bi2[:, :], in0=b2sb[:, :], in1=iv2[:, 0:NCK],
                                 op=ALU.mult)
            vsp(vector)
            vector.tensor_scalar(out=bi2[:, :], in0=bi2[:, :], scalar1=MAGIC,
                                 scalar2=MAGIC, op0=ALU.add, op1=ALU.subtract)
            vsp(vector)
            vector.tensor_tensor(out=bi2[:, :], in0=bi2[:, :], in1=sc2[:, :],
                                 op=ALU.mult)
            sV.inc(vector.tensor_copy(out=colsc[:, 12:13], in_=colsc[:, 7:8]))
        prep2V = sV.n

        @blk.sync
        def _(sync):
            sync.wait_ge(sG.h, gwait_codes)
            for fc in range(NFT):
                sL.inc(sync.dma_start_transpose(w2sb[:, fc * 1024:(fc + 1) * 1024],
                                                w2c_d[:, fc * 128:(fc + 1) * 128]), 16)
        w2L = sL.n

        baseLh, baseS, baseV2 = sL.n, sS.n, sV.n
        baseM2, baseE2, baseX, baseR = sM.n, sE.n, sX.n, sR.n

        @blk.sync
        def _(sync):
            for tb in range(globals()['_FFN2TBS']):
                for fc in range(NFT):
                    i = tb * NFT + fc
                    if i >= 4:
                        tb2, fc2 = divmod(i - 4, NFT)
                        sync.wait_ge(sS.h, baseS + 32 * tb2 + fc2 + 1)
                    sL.inc(sync.dma_start(
                        out=hbuf[:, (i % 4) * TBW:(i % 4 + 1) * TBW],
                        in_=h_d[tb, fc * 128:(fc + 1) * 128, :]), 16)

        for tb in range(globals()['_FFN2TBS']):
            @blk.scalar
            def _(scalar, tb=tb):
                if tb == 0:
                    scalar.wait_ge(sV.h, prep2V)
                for fc in range(NFT):
                    i = tb * NFT + fc
                    if i >= 2:
                        scalar.wait_ge(sV.h, baseV2 + i - 1)
                    scalar.wait_ge(sL.h, baseLh + 16 * (i + 1))
                    sS.inc(scalar.activation(
                        out=ust[:, (i % 2) * TBW:(i % 2 + 1) * TBW],
                        in_=hbuf[:, (i % 4) * TBW:(i % 4 + 1) * TBW],
                        func=AF.Identity, scale=colsc[:, 7:8], bias=colsc[:, 8:9]))

            @blk.vector
            def _(vector, tb=tb):
                for fc in range(NFT):
                    i = tb * NFT + fc
                    vector.wait_ge(sS.h, baseS + 32 * tb + fc + 1)
                    if i >= 2:
                        vector.wait_ge(sX.h, baseX + i - 1)
                    sV.inc(vector.tensor_single_scalar(
                        out=c2st[:, (i % 2) * TBW:(i % 2 + 1) * TBW],
                        in_=ust[:, (i % 2) * TBW:(i % 2 + 1) * TBW],
                        scalar=MAGIC, op=ALU.subtract))

            @blk.tensor
            def _(tensor, tb=tb):
                if tb == 0:
                    tensor.wait_ge(sL.h, w2L)
                    tensor.wait_ge(sV.h, prep2V)
                else:
                    tensor.wait_ge(sE.h, baseE2 + 8 * tb)
                for fc in range(NFT):
                    i = tb * NFT + fc
                    tensor.wait_ge(sV.h, baseV2 + 32 * tb + fc + 1)
                    for dt in range(NCK):
                        inst = tensor.matmul(
                            bank(dt),
                            w2sb[:, fc * 1024 + dt * 128:fc * 1024 + (dt + 1) * 128],
                            c2st[:, (i % 2) * TBW:(i % 2 + 1) * TBW],
                            start=(fc == 0), stop=(fc == NFT - 1))
                        if dt == NCK - 1:
                            sX.inc(inst)
                        elif fc == NFT - 1:
                            sM.inc(inst)

            @blk.scalar
            def _(scalar, tb=tb):
                for dt in range(NCK):
                    if dt < NCK - 1:
                        scalar.wait_ge(sM.h, baseM2 + tb * (NCK - 1) + dt + 1)
                    else:
                        scalar.wait_ge(sX.h, baseX + 32 * (tb + 1))
                    j = tb * NCK + dt
                    if j >= 2:
                        scalar.wait_ge(sR.h, baseR + j - 1)
                    sE.inc(scalar.activation(
                        out=fst[:, (dt % 2) * TBW:(dt % 2 + 1) * TBW], in_=bank(dt),
                        func=AF.Identity, scale=sc2[:, dt:dt + 1], bias=bi2[:, dt:dt + 1]))

            @blk.vector
            def _(vector, tb=tb):
                for dt in range(NCK):
                    vector.wait_ge(sE.h, baseE2 + tb * NCK + dt + 1)
                    sR.inc(vector.tensor_tensor(
                        out=Rsb[:, dt * T + tb * TBW:dt * T + (tb + 1) * TBW],
                        in0=Rsb[:, dt * T + tb * TBW:dt * T + (tb + 1) * TBW],
                        in1=fst[:, (dt % 2) * TBW:(dt % 2 + 1) * TBW], op=ALU.add))
        return sR.n

    Wb16 = {n: nc.dram_tensor(n + "b16", [H, H], BF16).ap()
            for n in ("Wq", "Wk", "Wv", "Wo")}
    o2_d = nc.dram_tensor("o2_d", [H, T], BF16).ap()

    def emit_wcast():
        with nc.sbuf_tensor("wtmp", [128, 2 * 1024], BF16) as wtmp:
            srcs = [("Wq", WQ), ("Wk", WK), ("Wv", WV), ("Wo", WO)]
            baseG = sG.n
            @blk.gpsimd
            def _(g):
                i = 0
                for nm, Wap in srcs:
                    for dc in range(NCK):
                        if i >= 2:
                            g.wait_ge(sG.h, baseG + 32 * (i - 1))
                        sG.inc(g.dma_start(out=wtmp[:, (i % 2) * 1024:(i % 2 + 1) * 1024],
                                           in_=Wap[dc * 128:(dc + 1) * 128, :]), 16)
                        g.wait_ge(sG.h, baseG + 32 * i + 16)
                        sG.inc(g.dma_start(out=Wb16[nm][dc * 128:(dc + 1) * 128, :],
                                           in_=wtmp[:, (i % 2) * 1024:(i % 2 + 1) * 1024]), 16)
                        i += 1
        return sG.n

    def emit_proj(pname, ln2V, aq, esA, DcBox, xcode):
        Wb = Wb16[{"q": "Wq", "k": "Wk", "v": "Wv"}[pname]]
        Aap = {"q": AQ, "k": None, "v": AV_}[pname]
        Bap = {"q": BQ, "k": None, "v": BV_}[pname]
        tA = {"q": CN["qC1"], "k": CN["kC1"], "v": CN["vphi"]}[pname]
        tB = {"q": CN["qC2"], "k": CN["kC2"], "v": None}[pname]
        es_p = ExitStack()
        sb3 = lambda name, sh, dt: es_p.enter_context(
            nc.sbuf_tensor(name + pname, sh, dt))
        wring = sb3("wring", [128, 2 * 1024], BF16)
        tbl1 = sb3("tbl1", [128, T], BF16)
        tbl2 = t2s = t3s = absb = bbsb = usb = None
        kshb = PTt = prodb = st4a = st4b = kvt = sqs = None
        if tB is not None:
            tbl2 = sb3("tbl2", [128, T], BF16)
            t2s = sb3("t2s", [128, T], F32)
            t3s = sb3("t3s", [128, T], F32)
        if Aap is not None:
            absb = sb3("absb", [128, NCK * LR], BF16)
            bbsb = sb3("bbsb", [64, 1024], BF16)
            usb = sb3("usb", [64, T], BF16)
        if pname in ("k", "v"):
            kshb = sb3("kshb", [128, T], BF16)
            st4a = sb3("st4a", [128, TBW], BF16)
            st4b = sb3("st4b", [128, TBW], BF16)
            kvt = sb3("kvt", [128, TBW], BF16)
        if pname == "k":
            PTt = sb3("PTt", [128, T], BF16)
            prodb = sb3("prodb", [128, T], BF16)
        Dc = DcBox[0]

        @blk.sync
        def _(sync):
            sync.wait_ge(sK.h, sK.n)
            sync.wait_ge(sV.h, ln2V)
            sync.wait_ge(sG.h, sG.n)
            sL.inc(sync.dma_start(out=tbl1[:, :], in_=tA[:, :]), 16)
            if tB is not None:
                sL.inc(sync.dma_start(out=tbl2[:, :], in_=tB[:, :]), 16)
        tblL = sL.n

        if Aap is not None:
            @blk.gpsimd
            def _(g):
                g.wait_ge(sV.h, ln2V)
                for dc in range(NCK):
                    sG.inc(g.dma_start(out=absb[:, dc * LR:(dc + 1) * LR],
                                       in_=Aap[dc * 128:(dc + 1) * 128, :]), 16)
                sG.inc(g.dma_start(out=bbsb[:, :], in_=Bap[:, :]), 16)
            abG = sG.n

            @blk.tensor
            def _(tensor):
                tensor.wait_ge(sG.h, abG)
                tensor.wait_ge(sV.h, ln2V)
                tensor.wait_ge(sR.h, sR.n)
                tensor.wait_ge(sK.h, sK.n)
                tensor.wait_ge(sE.h, sE.n)
                for tb in range(NTB):
                    last = None
                    for dc in range(NCK):
                        last = tensor.matmul(
                            pb[0:64, (4 + tb) * TBW:(5 + tb) * TBW],
                            absb[:, dc * LR:(dc + 1) * LR],
                            xcode[:, dc * T + tb * TBW:dc * T + (tb + 1) * TBW],
                            start=(dc == 0), stop=(dc == NCK - 1))
                    sM.inc(last)
            uM = sM.n

            @blk.scalar
            def _(scalar):
                scalar.wait_ge(sM.h, uM)
                for tb in range(NTB):
                    sE.inc(scalar.copy(out=usb[0:64, tb * TBW:(tb + 1) * TBW],
                                       in_=pb[0:64, (4 + tb) * TBW:(5 + tb) * TBW]))

        for ft in range(NPT):
            @blk.sync
            def _(sync, ft=ft):
                if ft >= 2:
                    sync.wait_ge(sM.h, sM.n)
                sL.inc(sync.dma_start(
                    out=wring[:, (ft % 2) * 1024:(ft % 2 + 1) * 1024].rearrange(
                        "p (c f) -> p c f", c=NCK),
                    in_=Wb[:, ft * 128:(ft + 1) * 128].rearrange(
                        "(c p) f -> p c f", p=128)), 16)
            wL = sL.n

            @blk.tensor
            def _(tensor, ft=ft):
                tensor.wait_ge(sL.h, wL)
                if ft == 0:
                    tensor.wait_ge(sV.h, ln2V)
                    tensor.wait_ge(sE.h, sE.n)
                    tensor.wait_ge(sR.h, sR.n)
                tensor.wait_ge(sK.h, sK.n)
                tensor.wait_ge(sS.h, sS.n)
                tensor.wait_ge(sG.h, sG.n)
                st = 0
                for tb in range(NTB):
                    last = None
                    for dc in range(NCK):
                        last = tensor.matmul(
                            bank(st + tb),
                            wring[:, (ft % 2) * 1024 + dc * 128:(ft % 2) * 1024 + (dc + 1) * 128],
                            xcode[:, dc * T + tb * TBW:dc * T + (tb + 1) * TBW],
                            start=(dc == 0), stop=(dc == NCK - 1 and Aap is None))
                    if Aap is not None:
                        last = tensor.matmul(
                            bank(st + tb), bbsb[:, ft * 128:(ft + 1) * 128],
                            usb[0:64, tb * TBW:(tb + 1) * TBW],
                            start=False, stop=True)
                    sM.inc(last)
            mmM = sM.n

            @blk.vector
            def _(vector, ft=ft):
                vector.wait_ge(sM.h, mmM)
                vector.wait_ge(sL.h, tblL)
                st = 0
                dst = aq if pname == "q" else kshb
                for tb in range(NTB):
                    bk = bank(st + tb)
                    cs_ = slice(tb * TBW, (tb + 1) * TBW)
                    dsl = (dst[:, ft * T + tb * TBW:ft * T + (tb + 1) * TBW]
                           if pname == "q" else dst[:, cs_])
                    if tB is not None:
                        vector.tensor_tensor(out=t2s[0:32, cs_], in0=bk[32:64, :],
                                             in1=tbl2[0:32, cs_], op=ALU.mult)
                        vector.tensor_tensor(out=t2s[32:64, cs_], in0=bk[0:32, :],
                                             in1=tbl2[32:64, cs_], op=ALU.mult)
                        vector.tensor_tensor(out=t2s[64:96, cs_], in0=bk[96:128, :],
                                             in1=tbl2[64:96, cs_], op=ALU.mult)
                        vector.tensor_tensor(out=t2s[96:128, cs_], in0=bk[64:96, :],
                                             in1=tbl2[96:128, cs_], op=ALU.mult)
                        vector.tensor_tensor(out=t3s[:, cs_], in0=bk[:, :],
                                             in1=tbl1[:, cs_], op=ALU.mult)
                        inst = vector.tensor_tensor(out=dsl, in0=t3s[:, cs_],
                                                    in1=t2s[:, cs_], op=ALU.add)
                    else:
                        inst = vector.tensor_tensor(out=dsl, in0=bk[:, :],
                                                    in1=tbl1[:, cs_], op=ALU.mult)
                    if tb == NTB - 1:
                        sK.inc(inst)
            ropeK = sK.n

            if pname == "k":
                @blk.vector
                def _(vector, ft=ft):
                    vector.wait_ge(sG.h, sG.n)
                    ksl = lambda j: kshb[:, j:T:4]
                    vector.tensor_tensor(out=st4a[:, :], in0=ksl(0), in1=ksl(1),
                                         op=ALU.add)
                    vector.tensor_tensor(out=st4b[:, :], in0=ksl(2), in1=ksl(3),
                                         op=ALU.add)
                    vector.tensor_tensor(out=kvt[:, :], in0=st4a[:, :], in1=st4b[:, :],
                                         op=ALU.add)
                    vector.tensor_copy(out=PTt[:, 0:T:4], in_=ksl(0))
                    vector.tensor_copy(out=PTt[:, 1:T:4], in_=st4a[:, :])
                    vector.tensor_tensor(out=st4b[:, :], in0=st4a[:, :], in1=ksl(2),
                                         op=ALU.add)
                    vector.tensor_copy(out=PTt[:, 2:T:4], in_=st4b[:, :])
                    vector.tensor_copy(out=PTt[:, 3:T:4], in_=kvt[:, :])
                    vsp(vector)
                    sK.inc(vector.tensor_tensor(out=prodb[:, :],
                                                in0=aq[:, ft * T:(ft + 1) * T],
                                                in1=PTt[:, :], op=ALU.mult))
                kpostK = sK.n

                @blk.gpsimd
                def _(g, ft=ft):
                    g.wait_ge(sK.h, kpostK)
                    sG.inc(g.dma_start(out=kv_in[ft * 128:(ft + 1) * 128, :],
                                       in_=kvt[:, :]), 16)

                @blk.tensor
                def _(tensor, ft=ft):
                    tensor.wait_ge(sK.h, kpostK)
                    last = None
                    for tb in range(NTB):
                        last = tensor.matmul(pb[0:16, (4 + tb) * TBW:(5 + tb) * TBW],
                                             csb["blk16"][:, ft * 16:(ft + 1) * 16],
                                             prodb[:, tb * TBW:(tb + 1) * TBW],
                                             start=(ft == 0), stop=(ft == NPT - 1))
                    sM.inc(last)
                if ft == NPT - 1:
                    @blk.scalar
                    def _(scalar):
                        scalar.wait_ge(sM.h, sM.n)
                        for tb in range(NTB):
                            sS.inc(scalar.copy(out=csp[0:16, tb * TBW:(tb + 1) * TBW],
                                               in_=pb[0:16, (4 + tb) * TBW:(5 + tb) * TBW]))
            elif pname == "v":
                @blk.vector
                def _(vector, ft=ft):
                    vector.wait_ge(sG.h, sG.n)
                    vsl = lambda j: kshb[:, j:T:4]
                    vector.tensor_tensor(out=st4a[:, :], in0=vsl(0), in1=vsl(1),
                                         op=ALU.add)
                    vector.tensor_tensor(out=st4b[:, :], in0=vsl(2), in1=vsl(3),
                                         op=ALU.add)
                    vector.tensor_tensor(out=kvt[:, :], in0=st4a[:, :], in1=st4b[:, :],
                                         op=ALU.add)
                    dco = ft * 1536
                    vector.tensor_tensor(out=Dc[:, dco:dco + TBW], in0=vsl(0),
                                         in1=kvt[:, :], op=ALU.subtract)
                    vector.tensor_tensor(out=Dc[:, dco + TBW:dco + 2 * TBW],
                                         in0=st4a[:, :], in1=kvt[:, :], op=ALU.subtract)
                    vector.tensor_tensor(out=st4b[:, :], in0=st4a[:, :], in1=vsl(2),
                                         op=ALU.add)
                    sK.inc(vector.tensor_tensor(out=Dc[:, dco + 2 * TBW:dco + 3 * TBW],
                                                in0=st4b[:, :], in1=kvt[:, :],
                                                op=ALU.subtract))
                vpostK = sK.n

                @blk.gpsimd
                def _(g, ft=ft):
                    g.wait_ge(sK.h, vpostK)
                    sG.inc(g.dma_start(out=kv_in[(NPT + ft) * 128:(NPT + ft + 1) * 128, :],
                                       in_=kvt[:, :]), 16)
        es_p.close()
        return

    def emit_attention(ln2V, aq, esA, xcode):
        DcBox = [None]
        emit_proj("q", ln2V, aq, esA, DcBox, xcode)
        emit_proj("k", ln2V, aq, esA, DcBox, xcode)
        DcBox[0] = esA.enter_context(nc.sbuf_tensor("Dc", [128, NPT * 1536], BF16, side="right"))
        emit_proj("v", ln2V, aq, esA, DcBox, xcode)
        Dc = DcBox[0]

        @blk.gpsimd
        def _(g):
            g.wait_ge(sG.h, sG.n)
            g.wait_ge(sK.h, sK.n)
            sC.inc(g.collective_compute(
                "AllGather", ALU.bypass,
                replica_groups=[[0, 1], [2, 3], [4, 5], [6, 7]],
                ins=[kv_in], outs=[kv_out]))
        agC = sC.n
        return agC, Dc

    wcastG = emit_wcast()
    emit_wquant(wcastG)
    wcodesG = sG.n
    if _STOP == "wq":
        @blk.gpsimd
        def _(g):
            g.wait_ge(sG.h, wcodesG)
            g.wait_ge(sL.h, sL.n)
            for c in range(NCK):
                sG.inc(g.dma_start(out=yT[c * 128:(c + 1) * 128, :],
                                   in_=Rsb[:, c * T:(c + 1) * T]), 16)
            g.wait_ge(sG.h, sG.n)
        blk = _NullBlk()

    def half_ffn(tag, xcode):
        nonlocal blk
        with (nc.sbuf_tensor("lnfull" + tag, [128, NCK * T], F32) as lnfull,
              nc.sbuf_tensor("lnb" + tag, [128, 2 * T], BF16) as lnb,
              nc.sbuf_tensor("sqb" + tag, [128, 2 * T], BF16) as sqb,
              nc.sbuf_tensor("pool1" + tag, [128, T], F32) as pool1):
            mxV = emit_ln_stats_affine(pool1, lnb, sqb, None, lnfull, xcode, quant=True)
            gW = emit_allreduce_max(mxV)
            q2V = emit_ln_pass2(lnfull, xcode, gW)
            if _STOP == "lnq" + tag:
                @blk.gpsimd
                def _(g):
                    g.wait_ge(sV.h, q2V)
                    for c in range(NCK):
                        sG.inc(g.dma_start(out=yT[c * 128:(c + 1) * 128, :],
                                           in_=Rsb[:, c * T:(c + 1) * T]), 16)
                    g.wait_ge(sG.h, sG.n)
                blk = _NullBlk()
                pass
        with (nc.sbuf_tensor("w1sb" + tag, [128, NCK * FF], BF16) as w1sb,
              nc.sbuf_tensor("hst" + tag, [128, 2 * T], F16) as hst):
            @blk.sync
            def _(sync):
                sync.wait_ge(sV.h, q2V)
            hmaxV, hG = emit_ffn1(w1sb, hst, xcode, wcodesG, q2V)
            if _STOP == "ffn1" + tag:
                @blk.gpsimd
                def _(g):
                    g.wait_ge(sG.h, hG)
                    for c in range(NCK):
                        sG.inc(g.dma_start(out=yT[c * 128:(c + 1) * 128, :],
                                           in_=Rsb[:, c * T:(c + 1) * T]), 16)
                    g.wait_ge(sG.h, sG.n)
                blk = _NullBlk()
        gW2 = emit_allreduce_max(hmaxV)
        if _STOP == "m2" + tag:
            @blk.gpsimd
            def _(g):
                g.wait_ge(sG.h, gW2)
                for c in range(NCK):
                    sG.inc(g.dma_start(out=yT[c * 128:(c + 1) * 128, :],
                                       in_=Rsb[:, c * T:(c + 1) * T]), 16)
            blk = _NullBlk()
        with (nc.sbuf_tensor("w2sb" + tag, [128, NFT * 1024], BF16) as w2sb,
              nc.sbuf_tensor("hring" + tag, [128, 4 * TBW], F16) as hring,
              nc.sbuf_tensor("ust" + tag, [128, 2 * TBW], F32) as ust,
              nc.sbuf_tensor("c2st" + tag, [128, 2 * TBW], BF16) as c2st,
              nc.sbuf_tensor("fst" + tag, [128, 2 * TBW], F32) as fst):
            @blk.sync
            def _(sync):
                sync.wait_ge(sG.h, hG)
            raddV = emit_ffn2(w2sb, hring, ust, c2st, fst, wcodesG, gW2)
        return raddV

    esX1 = ExitStack()
    xcodeA = esX1.enter_context(nc.sbuf_tensor("xcodeA", [128, NCK * T], BF16))
    r1V = half_ffn("a", xcodeA)

    if _STOP == "ffna":
        @blk.gpsimd
        def _(g):
            g.wait_ge(sR.h, r1V)
            for c in range(NCK):
                sG.inc(g.dma_start(out=yT[c * 128:(c + 1) * 128, :],
                                   in_=Rsb[:, c * T:(c + 1) * T]), 16)
            if _DEBUG:
                sG.inc(g.dma_start(out=dbgt["d_a"][:, :], in_=xcodeA[:, :]), 16)
                sG.inc(g.dma_start(out=dbgt["d_b"][0:128, 0:T], in_=pool2[:, :]), 16)
                sG.inc(g.dma_start(out=dbgt["d_b"][0:128, T:T + 16], in_=colsc[:, :]), 16)
                sG.inc(g.dma_start(out=dbgt["d_b"][0:128, T + 16:T + 48], in_=sc1[:, :]), 16)
                sG.inc(g.dma_start(out=dbgt["d_b"][0:128, T + 48:T + 80], in_=bi1[:, :]), 16)
                sG.inc(g.dma_start(out=dbgt["d_b"][0:128, T + 80:T + 112], in_=am1[:, :]), 16)
            g.wait_ge(sG.h, sG.n)
        blk = _NullBlk()

    with (nc.sbuf_tensor("lnbq", [128, 2 * T], BF16) as lnb,
          nc.sbuf_tensor("sqbq", [128, 2 * T], BF16) as sqb,
          nc.sbuf_tensor("lnscrq", [128, TBW], F32) as lnscr,
          nc.sbuf_tensor("pool1q", [128, T], F32) as pool1):
        @blk.scalar
        def _(scalar):
            scalar.wait_ge(sR.h, r1V)
        ln2V = emit_ln_stats_affine(pool1, lnb, sqb, lnscr, None, xcodeA, quant=False)

    if _STOP == "ln2":
        @blk.gpsimd
        def _(g):
            g.wait_ge(sV.h, ln2V)
            for c in range(NCK):
                sG.inc(g.dma_start(out=yT[c * 128:(c + 1) * 128, :],
                                   in_=Rsb[:, c * T:(c + 1) * T]), 16)
            g.wait_ge(sG.h, sG.n)
        blk = _NullBlk()

    esA = ExitStack()
    aq = esA.enter_context(nc.sbuf_tensor("aq", [128, NPT * T], BF16, side="right"))
    agC, Dc = emit_attention(ln2V, aq, esA, xcodeA)

    if _STOP == "proj":
        @blk.gpsimd
        def _(g):
            g.wait_ge(sC.h, agC)
            g.wait_ge(sK.h, sK.n)
            g.wait_ge(sM.h, sM.n)
            for c in range(NCK):
                sG.inc(g.dma_start(out=yT[c * 128:(c + 1) * 128, :],
                                   in_=Rsb[:, c * T:(c + 1) * T]), 16)
            if _DEBUG:
                sG.inc(g.dma_start(out=dbgt["d_a"][:, :], in_=aq[:, :]), 16)
                sG.inc(g.dma_start(out=dbgt["d_b"][:, :], in_=xcodeA[:, :]), 16)
                for i in range(32):
                    sG.inc(g.dma_start(out=dbgt["d_c"][0:128, i * SLO:(i + 1) * SLO],
                                       in_=kv_out[i * 128:(i + 1) * 128, :]), 16)
                for i in range(NCK):
                    sG.inc(g.dma_start(out=dbgt["d_d"][0:128, i * 1024:(i + 1) * 1024],
                                       in_=Wb16["Wk"][i * 128:(i + 1) * 128, :]), 16)
                for i in range(16):
                    sG.inc(g.dma_start(
                        out=dbgt["d_d"][0:128, 8192 + i * SLO:8192 + (i + 1) * SLO],
                        in_=kv_in[i * 128:(i + 1) * 128, :]), 16)
            g.wait_ge(sG.h, sG.n)
        blk = _NullBlk()
    esX1.close()

    for n, sh, dt in (("oneh", [16, NH * 128], BF16), ("mA_past", [128, 512], BF16),
                      ("mA_eq", [128, 512], BF16), ("mB_past", [128, 512], BF16),
                      ("mB_eq", [128, 512], BF16), ("phi2", [64, T], BF16)):
        csb[n] = esA.enter_context(nc.sbuf_tensor("c_" + n, sh, dt, side="right"))
    ecsb = esA.enter_context(nc.sbuf_tensor("ecsb", [16, T], BF16, side="right"))
    ncbank = esA.enter_context(nc.sbuf_tensor("ncbank", [16, NS], BF16, side="right"))
    @blk.sync
    def _(sync):
        sync.wait_ge(sK.h, sK.n)
        sync.wait_ge(sG.h, sG.n)
        sync.wait_ge(sS.h, sS.n)
        for n in ("oneh", "mA_past", "mA_eq", "mB_past", "mB_eq", "phi2"):
            sL.inc(sync.dma_start(out=csb[n][:, :], in_=CN[n][:, :]), 16)
    @blk.gpsimd
    def _(g):
        g.wait_ge(sK.h, sK.n)
        for r in range(16):
            sG.inc(g.dma_start(out=ncbank[r:r + 1, :], in_=CN["slotmask"][:, :]), 16)
    cstL, cstG = sL.n, sG.n
    KTf = esA.enter_context(nc.sbuf_tensor("KTf", [128, NPT * NS], BF16, side="right"))
    Vsm = esA.enter_context(nc.sbuf_tensor("Vsm", [128, NPT * 8 * 130], BF16, side="right"))
    esV = ExitStack()
    VTf = esV.enter_context(nc.sbuf_tensor("VTf", [128, NPT * NS], BF16, side="right"))

    @blk.gpsimd
    def _(g):
        g.wait_ge(sC.h, agC)
        for pt in range(NPT):
            for zz in range(2):
                sG.inc(g.dma_start(
                    out=KTf[:, pt * NS + zz * SLO:pt * NS + (zz + 1) * SLO],
                    in_=kv_out[zz * 2048 + pt * 128:zz * 2048 + (pt + 1) * 128, :]), 16)
                sG.inc(g.dma_start(
                    out=VTf[:, pt * NS + zz * SLO:pt * NS + (zz + 1) * SLO],
                    in_=kv_out[zz * 2048 + 1024 + pt * 128:
                               zz * 2048 + 1024 + (pt + 1) * 128, :]), 16)
    kvG = sG.n

    if _STOP == "ag":
        @blk.gpsimd
        def _(g):
            g.wait_ge(sG.h, kvG)
            for c in range(NCK):
                sG.inc(g.dma_start(out=yT[c * 128:(c + 1) * 128, :],
                                   in_=Rsb[:, c * T:(c + 1) * T]), 16)
            g.wait_ge(sG.h, sG.n)
        blk = _NullBlk()

    with nc.sbuf_tensor("vtstg", [128, 2 * 128], F32) as vtstg:
        @blk.vector
        def _(vector):
            vector.wait_ge(sM.h, sM.n)
            vector.wait_ge(sK.h, sK.n)
            vector.wait_ge(sC.h, agC)
            vector.memset(Vsm[:, 64:NPT * 8 * 130:130], 1.0)
            sV.inc(vector.memset(Vsm[:, 129:NPT * 8 * 130:130], 1.0))
        vsmI = sV.n
        bS, bM, bE = sS.n, sM.n, sE.n

        @blk.scalar
        def _(scalar):
            scalar.wait_ge(sG.h, kvG)
            scalar.wait_ge(sV.h, vsmI)
            for i in range(NPT * 8 + 1):
                if i < NPT * 8:
                    pt, stt = i // 8, i % 8
                    if i >= 2:
                        scalar.wait_ge(sM.h, bM + i - 1)
                    sS.inc(scalar.copy(
                        out=vtstg[:, (i % 2) * 128:(i % 2 + 1) * 128],
                        in_=VTf[:, pt * NS + stt * 128:pt * NS + (stt + 1) * 128]))
                if i >= 1:
                    j = i - 1
                    scalar.wait_ge(sM.h, bM + j + 1)
                    jb = 2 + (j % 2)
                    jpt, jst = j // 8, j % 8
                    base = jpt * 1040 + jst * 130
                    scalar.copy(out=Vsm[:, base:base + 64],
                                in_=pb[:, jb * TBW:jb * TBW + 64])
                    sE.inc(scalar.copy(out=Vsm[:, base + 65:base + 129],
                                       in_=pb[:, jb * TBW + 64:jb * TBW + 128]))

        @blk.tensor
        def _(tensor):
            for i in range(NPT * 8):
                tensor.wait_ge(sS.h, bS + i + 1)
                if i >= 2:
                    tensor.wait_ge(sE.h, bE + i - 1)
                sM.inc(tensor.transpose(
                    pb[:, (2 + (i % 2)) * TBW:(2 + (i % 2)) * TBW + 128],
                    vtstg[:, (i % 2) * 128:(i % 2 + 1) * 128],
                    csb["identf"][:, :]))
    vsmE = sE.n
    esV.close()

    @blk.vector
    def _(vector):
        vector.wait_ge(sS.h, sS.n)
        vector.reduce_max(out=ch8[0:16, 6:7], in_=csp[0:16, :], axis=AX.X)
        vsp(vector)
        vector.tensor_scalar(out=nc16[:, 0:1], in0=ch8[0:16, 6:7],
                             scalar1=-1.0, scalar2=None, op0=ALU.mult)
        vsp(vector)
        vector.tensor_copy(out=ncf[:, 0:1], in_=nc16[:, 0:1])
        vsp(vector)
        vector.wait_ge(sG.h, cstG)
        sV.inc(vector.tensor_scalar(out=ncbank[0:16, :], in0=ncbank[0:16, :],
                                    scalar1=ncf[0:16, 0:1], scalar2=None, op0=ALU.add))
    ncV = sV.n
    @blk.scalar
    def _(scalar):
        scalar.wait_ge(sV.h, ncV)
        sS.inc(scalar.activation(out=ecsb[0:16, :], in_=csp[0:16, :], func=AF.Exp,
                                 bias=ncf[:, 0:1]))
    ecsS = sS.n

    def dump_attn(g):
        if not _DEBUG:
            return
        sG.inc(g.dma_start(out=dbgt["d_a"][:, :], in_=aq[:, :]), 16)
        sG.inc(g.dma_start(out=dbgt["d_b"][0:128, 0:T], in_=pool2[:, :]), 16)
        sG.inc(g.dma_start(out=dbgt["d_b"][0:128, T:T + 16], in_=colsc[:, :]), 16)
        sG.inc(g.dma_start(out=dbgt["d_b"][0:16, T + 16:2 * T + 16], in_=ecsb[0:16, :]), 16)
        sG.inc(g.dma_start(out=dbgt["d_b"][0:16, 2 * T + 16:2 * T + 20], in_=chead[0:16, :]), 16)
        sG.inc(g.dma_start(out=dbgt["d_b"][0:16, 2 * T + 20:2 * T + 20 + NS],
                           in_=ncbank[0:16, :]), 16)
        sG.inc(g.dma_start(out=dbgt["d_b"][0:16, 2 * T + 20 + NS:3 * T + 20 + NS],
                           in_=csp[0:16, :]), 16)
        for i in range(32):
            sG.inc(g.dma_start(out=dbgt["d_c"][0:128, i * SLO:(i + 1) * SLO],
                               in_=kv_out[i * 128:(i + 1) * 128, :]), 16)
        for i in range(NCK):
            sG.inc(g.dma_start(out=dbgt["d_d"][0:128, i * T:(i + 1) * T],
                               in_=o2_d[i * 128:(i + 1) * 128, :]), 16)

    if _STOP == "ecs":
        @blk.gpsimd
        def _(g):
            g.wait_ge(sS.h, ecsS)
            for c in range(NCK):
                sG.inc(g.dma_start(out=yT[c * 128:(c + 1) * 128, :],
                                   in_=Rsb[:, c * T:(c + 1) * T]), 16)
            dump_attn(g)
            g.wait_ge(sG.h, sG.n)
        blk = _NullBlk()

    ETR = 3
    efix = esA.enter_context(nc.sbuf_tensor("efix", [128, TBW], BF16, side="right"))
    o2t = esA.enter_context(nc.sbuf_tensor("o2t", [128, T], BF16, side="right"))
    kst = esA.enter_context(nc.sbuf_tensor("kst", [65, NS], BF16, side="right"))
    ast = esA.enter_context(nc.sbuf_tensor("ast", [65, T], BF16, side="right"))
    et = esA.enter_context(nc.sbuf_tensor("et", [128, ETR * TBW], BF16, side="right"))
    ctb = esA.enter_context(nc.sbuf_tensor("ctb", [64, TBW], BF16, side="right"))
    ev1 = esA.enter_context(nc.sbuf_tensor("ev1", [64, TBW], F32, side="right"))

    @blk.vector
    def _(vector):
        vector.wait_ge(sE.h, vsmE)
        vector.memset(ctb[:, :], 0.0)
        sV.inc(vector.memset(ast[64:65, :], 1.0))
    astI = sV.n

    SCW = [(st, tb) for st in range(NST) for tb in range(max(0, st - 4), NTB)]
    o2spillG = [sG.n]

    for hh in range(NH):
        pt, half = hh // 2, hh % 2
        rowoff = 64 * half
        cnt = {}

        @blk.scalar
        def _(scalar, hh=hh, pt=pt, rowoff=rowoff):
            scalar.wait_ge(sG.h, kvG)
            scalar.wait_ge(sM.h, sM.n)
            scalar.copy(out=kst[0:64, :],
                        in_=KTf[rowoff:rowoff + 64, pt * NS:(pt + 1) * NS])
            sE.inc(scalar.copy(out=ast[0:64, :],
                               in_=aq[rowoff:rowoff + 64, pt * T:(pt + 1) * T]))
        stageE = sE.n

        @blk.sync
        def _(sync, hh=hh):
            sync.wait_ge(sS.h, ecsS)
            sync.wait_ge(sV.h, astI)
            sync.wait_ge(sM.h, sM.n)
            sL.inc(sync.dma_start(out=kst[64:65, :], in_=ncbank[hh:hh + 1, :]), 16)
        stageL = sL.n

        kpred = {}
        kc = sK.n
        for (st, tb) in SCW:
            if st == tb or st == tb + 4:
                kc += 1
                kpred[("fix", st, tb)] = kc
            if st == tb + 4:
                kc += 1
                kpred[("zprep", tb)] = kc

        expbase = sS.n
        rbase = sR.n
        xbase = sX.n

        @blk.tensor
        def _(tensor, hh=hh, pt=pt, half=half, cnt=cnt, kpred=kpred,
              expbase=expbase, rbase=rbase, stageE=stageE, stageL=stageL):
            tensor.wait_ge(sL.h, stageL)
            tensor.wait_ge(sE.h, stageE)
            tensor.wait_ge(sR.h, rbase)
            prev_bce = [None]
            for i, (st, tb) in enumerate(SCW):
                if i >= 2:
                    tensor.wait_ge(sS.h, expbase + i - 1)
                sM.inc(tensor.matmul(
                    bank(4 + i % 2),
                    kst[:, st * 128:(st + 1) * 128],
                    ast[:, tb * TBW:(tb + 1) * TBW],
                    start=True, stop=True))
                cnt[("sc", i)] = sM.n
                if st == tb or st == tb + 4:
                    if prev_bce[0] is not None:
                        pst, ptb = prev_bce[0]
                        tensor.wait_ge(sK.h, kpred[("fix", pst, ptb)])
                        if pst == ptb + 4:
                            tensor.wait_ge(sR.h, rbase + ptb + 1)
                    sM.inc(tensor.matmul(pb[:, 6 * TBW:7 * TBW],
                                         csb["oneh"][:, hh * 128:(hh + 1) * 128],
                                         ecsb[0:16, tb * TBW:(tb + 1) * TBW],
                                         start=True, stop=True))
                    cnt[("bce", i)] = sM.n
                    prev_bce[0] = (st, tb)
                diag = (st == tb or st == tb + 4)
                if diag:
                    tensor.wait_ge(sK.h, kpred[("fix", st, tb)])
                else:
                    tensor.wait_ge(sS.h, expbase + i + 1)
                sX.inc(tensor.matmul(
                    pb[0:65, tb * TBW:(tb + 1) * TBW],
                    Vsm[:, pt * 1040 + st * 130 + half * 65:
                        pt * 1040 + st * 130 + half * 65 + 65],
                    et[:, (i % ETR) * TBW:(i % ETR + 1) * TBW],
                    start=(st == 0), stop=(st == tb + 4)))
                cnt[("av", i)] = sX.n
                if st == tb + 4:
                    tensor.wait_ge(sK.h, kpred[("zprep", tb)])
                    if tb >= 1:
                        tensor.wait_ge(sR.h, rbase + tb)
                    sM.inc(tensor.matmul(pb[0:64, 7 * TBW:7 * TBW + TBW],
                                         csb["onesf"][0:1, 0:64],
                                         pool2[0:1, tb * TBW:(tb + 1) * TBW],
                                         start=True, stop=True))
                    cnt[("zbc", tb)] = sM.n

        @blk.scalar
        def _(scalar, hh=hh, cnt=cnt, xbase=xbase):
            scalar.wait_ge(sX.h, xbase)
            for i, (st, tb) in enumerate(SCW):
                scalar.wait_ge(sM.h, cnt[("sc", i)])
                if i >= ETR:
                    scalar.wait_ge(sX.h, cnt[("av", i - ETR)])
                if tb < st < tb + 4:
                    sS.inc(scalar.activation(
                        out=et[:, (i % ETR) * TBW:(i % ETR + 1) * TBW],
                        in_=bank(4 + i % 2), func=AF.Exp, bias=csb["midb"][:, 0:1]))
                else:
                    sS.inc(scalar.activation(
                        out=et[:, (i % ETR) * TBW:(i % ETR + 1) * TBW],
                        in_=bank(4 + i % 2), func=AF.Exp))

        @blk.vector
        def _(vector, hh=hh, pt=pt, half=half, rowoff=rowoff, cnt=cnt,
              expbase=expbase):
            if half == 0 and pt >= 1:
                vector.wait_ge(sG.h, o2spillG[0])
            for i, (st, tb) in enumerate(SCW):
                if st == tb or st == tb + 4:
                    mp = csb["mA_past"] if st == tb else csb["mB_past"]
                    me = csb["mA_eq"] if st == tb else csb["mB_eq"]
                    vector.wait_ge(sS.h, expbase + i + 1)
                    vector.wait_ge(sM.h, cnt[("bce", i)])
                    esl = et[:, (i % ETR) * TBW:(i % ETR + 1) * TBW]
                    vector.tensor_tensor(out=esl, in0=esl, in1=mp[:, :], op=ALU.mult)
                    vector.tensor_tensor(out=efix[:, :], in0=pb[:, 6 * TBW:7 * TBW],
                                         in1=me[:, :], op=ALU.mult)
                    sK.inc(vector.tensor_tensor(out=esl, in0=esl, in1=efix[:, :],
                                                op=ALU.add))
                if st == tb + 4:
                    vector.wait_ge(sX.h, cnt[("av", i)])
                    vector.tensor_copy(out=pool2[0:1, tb * TBW:(tb + 1) * TBW],
                                       in_=pb[64:65, tb * TBW:(tb + 1) * TBW])
                    vsp(vector, 1)
                    sK.inc(vector.reciprocal(out=pool2[0:1, tb * TBW:(tb + 1) * TBW],
                                             in_=pool2[0:1, tb * TBW:(tb + 1) * TBW]))
                    vector.wait_ge(sM.h, cnt[("zbc", tb)])
                    for jj in range(3):
                        vector.tensor_tensor(
                            out=ctb[:, jj:TBW:4],
                            in0=Dc[rowoff:rowoff + 64,
                                   pt * 1536 + jj * TBW + tb * 128:
                                   pt * 1536 + jj * TBW + (tb + 1) * 128],
                            in1=pb[0:64, 6 * TBW + jj:7 * TBW:4], op=ALU.mult)
                    vsp(vector)
                    vector.tensor_tensor(out=ev1[:, :],
                                         in0=pb[0:64, tb * TBW:(tb + 1) * TBW],
                                         in1=ctb[:, :], op=ALU.add)
                    vector.tensor_tensor(out=ev1[:, :], in0=ev1[:, :],
                                         in1=pb[0:64, 7 * TBW:7 * TBW + TBW],
                                         op=ALU.mult)
                    sR.inc(vector.tensor_tensor(
                        out=o2t[rowoff:rowoff + 64, tb * TBW:(tb + 1) * TBW],
                        in0=ev1[:, :],
                        in1=csb["phi2"][0:64, tb * TBW:(tb + 1) * TBW],
                        op=ALU.mult))
        assert isinstance(blk, _NullBlk) or sK.n == kc, (hh, sK.n, kc)

        if half == 1:
            @blk.gpsimd
            def _(g, pt=pt):
                g.wait_ge(sR.h, sR.n)
                sG.inc(g.dma_start(out=o2_d[pt * 128:(pt + 1) * 128, :],
                                   in_=o2t[:, :]), 16)
            o2spillG[0] = sG.n
    headR = sR.n
    headSpill = sG.n

    if _STOP == "heads":
        @blk.gpsimd
        def _(g):
            g.wait_ge(sR.h, headR)
            g.wait_ge(sG.h, headSpill)
            for c in range(NCK):
                sG.inc(g.dma_start(out=yT[c * 128:(c + 1) * 128, :],
                                   in_=Rsb[:, c * T:(c + 1) * T]), 16)
            dump_attn(g)
            g.wait_ge(sG.h, sG.n)
        blk = _NullBlk()
    esA.close()

    with (nc.sbuf_tensor("o2full", [128, NCK * T], BF16) as o2full,
          nc.sbuf_tensor("wringo", [128, 2 * 1024], BF16) as wringo,
          nc.sbuf_tensor("absbo", [128, NCK * LR], BF16) as absbo,
          nc.sbuf_tensor("bbsbo", [64, 1024], BF16) as bbsbo,
          nc.sbuf_tensor("usbo", [64, T], BF16) as usbo):
        @blk.sync
        def _(sync):
            sync.wait_ge(sG.h, headSpill)
            sync.wait_ge(sR.h, headR)
            for dc in range(NCK):
                sL.inc(sync.dma_start(out=o2full[:, dc * T:(dc + 1) * T],
                                      in_=o2_d[dc * 128:(dc + 1) * 128, :]), 16)
        o2L = sL.n

        @blk.gpsimd
        def _(g):
            g.wait_ge(sR.h, headR)
            for dc in range(NCK):
                sG.inc(g.dma_start(out=absbo[:, dc * LR:(dc + 1) * LR],
                                   in_=AO[dc * 128:(dc + 1) * 128, :]), 16)
            sG.inc(g.dma_start(out=bbsbo[:, :], in_=BO[:, :]), 16)
        oG = sG.n

        @blk.tensor
        def _(tensor):
            tensor.wait_ge(sL.h, o2L)
            tensor.wait_ge(sG.h, oG)
            for tb in range(NTB):
                last = None
                for dc in range(NCK):
                    last = tensor.matmul(
                        pb[0:64, (4 + tb) * TBW:(5 + tb) * TBW],
                        absbo[:, dc * LR:(dc + 1) * LR],
                        o2full[:, dc * T + tb * TBW:dc * T + (tb + 1) * TBW],
                        start=(dc == 0), stop=(dc == NCK - 1))
                sM.inc(last)
        uoM = sM.n

        @blk.scalar
        def _(scalar):
            scalar.wait_ge(sM.h, uoM)
            for tb in range(NTB):
                sE.inc(scalar.copy(out=usbo[0:64, tb * TBW:(tb + 1) * TBW],
                                   in_=pb[0:64, (4 + tb) * TBW:(5 + tb) * TBW]))
        uoE = sE.n

        for ft in range(NCK):
            @blk.sync
            def _(sync, ft=ft):
                if ft >= 2:
                    sync.wait_ge(sM.h, sM.n)
                sL.inc(sync.dma_start(
                    out=wringo[:, (ft % 2) * 1024:(ft % 2 + 1) * 1024].rearrange(
                        "p (c f) -> p c f", c=NCK),
                    in_=Wb16["Wo"][:, ft * 128:(ft + 1) * 128].rearrange(
                        "(c p) f -> p c f", p=128)), 16)
            woL = sL.n

            @blk.tensor
            def _(tensor, ft=ft):
                tensor.wait_ge(sL.h, woL)
                if ft == 0:
                    tensor.wait_ge(sE.h, uoE)
                else:
                    tensor.wait_ge(sR.h, sR.n)
                st = (ft % 2) * 4
                for tb in range(NTB):
                    for dc in range(NCK):
                        tensor.matmul(
                            bank(st + tb),
                            wringo[:, (ft % 2) * 1024 + dc * 128:(ft % 2) * 1024 + (dc + 1) * 128],
                            o2full[:, dc * T + tb * TBW:dc * T + (tb + 1) * TBW],
                            start=(dc == 0), stop=False)
                    last = tensor.matmul(bank(st + tb), bbsbo[:, ft * 128:(ft + 1) * 128],
                                         usbo[0:64, tb * TBW:(tb + 1) * TBW],
                                         start=False, stop=True)
                    if tb == NTB - 1:
                        sM.inc(last)
            myM = sM.n

            @blk.vector
            def _(vector, ft=ft, myM=myM):
                vector.wait_ge(sM.h, myM)
                st = (ft % 2) * 4
                for tb in range(NTB):
                    sR.inc(vector.tensor_tensor(
                        out=Rsb[:, ft * T + tb * TBW:ft * T + (tb + 1) * TBW],
                        in0=Rsb[:, ft * T + tb * TBW:ft * T + (tb + 1) * TBW],
                        in1=bank(st + tb), op=ALU.add))
    yR = sR.n

    if _STOP == "yproj":
        @blk.gpsimd
        def _(g):
            g.wait_ge(sR.h, yR)
            for c in range(NCK):
                sG.inc(g.dma_start(out=yT[c * 128:(c + 1) * 128, :],
                                   in_=Rsb[:, c * T:(c + 1) * T]), 16)
            g.wait_ge(sG.h, sG.n)
        blk = _NullBlk()

    @blk.scalar
    def _(scalar):
        scalar.wait_ge(sR.h, yR)
    @blk.vector
    def _(vector):
        vector.wait_ge(sR.h, yR)
    @blk.sync
    def _(sync):
        sync.wait_ge(sR.h, yR)
    with nc.sbuf_tensor("xcodeB", [128, NCK * T], BF16) as xcodeB:
        r2V = half_ffn("b", xcodeB)

    @blk.gpsimd
    def _(g):
        g.wait_ge(sR.h, r2V)
        for c in range(NCK):
            sG.inc(g.dma_start(out=yT[c * 128:(c + 1) * 128, :],
                               in_=Rsb[:, c * T:(c + 1) * T]), 16)
        g.wait_ge(sG.h, sG.n)

    if _DEBUG:
        pass

    ctx.close()
    return nc


def kernel(L0_attn_hidden_states, L0_attn_Wq, L0_attn_Wk, L0_attn_Wv, L0_attn_Wo,
           L0_attn_Aq, L0_attn_Bq, L0_attn_Av, L0_attn_Bv, L0_attn_Ao, L0_attn_Bo,
           L0_ffn_w1, L0_ffn_b1, L0_ffn_w2, L0_ffn_b2):
    x = np.asarray(L0_attn_hidden_states, dtype=np.float32)
    shared = {
        "Wq": np.asarray(L0_attn_Wq, np.float32), "Wk": np.asarray(L0_attn_Wk, np.float32),
        "Wv": np.asarray(L0_attn_Wv, np.float32), "Wo": np.asarray(L0_attn_Wo, np.float32),
        "Aq": np.asarray(L0_attn_Aq, np.float32), "Av": np.asarray(L0_attn_Av, np.float32),
        "Ao": np.asarray(L0_attn_Ao, np.float32),
        "Bq": np.asarray(L0_attn_Bq, np.float32), "Bv": np.asarray(L0_attn_Bv, np.float32),
        "Bo": np.asarray(L0_attn_Bo, np.float32),
        "w1": np.asarray(L0_ffn_w1, np.float32), "w2": np.asarray(L0_ffn_w2, np.float32),
        "b1l": np.ascontiguousarray(np.asarray(L0_ffn_b1, np.float32).reshape(NFT, 128).T),
        "b2l": np.ascontiguousarray(np.asarray(L0_ffn_b2, np.float32).reshape(NCK, 128).T),
    }
    cc = [_host_consts(z) for z in (0, 1)]
    in_maps = []
    for c in range(8):
        b, z = c // 2, c % 2
        m = dict(shared)
        m.update(cc[z])
        m["xT"] = np.ascontiguousarray(x[b, 2048 * z:2048 * (z + 1), :].T)
        in_maps.append(m)

    nc = build_nc()
    res = run_bass_kernel_spmd(nc, in_maps, core_ids=list(range(8)))

    out = np.empty((B, S, H), dtype=np.float32)
    for c in range(8):
        b, z = c // 2, c % 2
        out[b, 2048 * z:2048 * (z + 1), :] = np.asarray(res.results[c]["yT"]).T
    return out
